# version 24
# baseline (speedup 1.0000x reference)
import numpy as np
import ml_dtypes
import concourse.bass as bass
import concourse.mybir as mybir
from concourse.bass_utils import run_bass_kernel_spmd
from contextlib import ExitStack

F32 = mybir.dt.float32
BF16 = mybir.dt.bfloat16
AF = mybir.ActivationFunctionType
ALU = mybir.AluOpType

S = 2048
D = 2048
HEADS = 16
HD = 128
HF = 8192
NT = 16
RSQ = float(np.float32(1.0 / np.sqrt(np.float64(HF))))

DBG = False


def _had(n):
    h = np.array([[1.0]], dtype=np.float32)
    while h.shape[0] < n:
        h = np.block([[h, h], [h, -h]])
    return h


class _Sem:
    __slots__ = ("h", "v")
    def __init__(self, h):
        self.h = h
        self.v = 0


class _Builder:
    def __init__(self):
        self.prog = {k: [] for k in "tvsyg"}
        self._waited = {k: {} for k in "tvsyg"}

    def raw(self, eng, fn):
        self.prog[eng].append(fn)

    def wait(self, eng, sem, val):
        if val is None or val <= 0:
            return
        key = id(sem.h)
        if self._waited[eng].get(key, -1) >= val:
            return
        self._waited[eng][key] = val
        self.prog[eng].append(lambda e, s=sem.h, v=val: e.wait_ge(s, v))

    def waits(self, eng, pairs):
        for sem, val in pairs:
            self.wait(eng, sem, val)

    def op(self, eng, meth, *args, inc=None, incn=1, **kw):
        tgt = None
        ret = None
        if inc is not None:
            inc.v += incn
            tgt = (inc.h, incn)
            ret = inc.v
        def f(e, meth=meth, args=args, kw=kw, tgt=tgt):
            r = getattr(e, meth)(*args, **kw)
            if tgt is not None:
                r.then_inc(*tgt)
        self.prog[eng].append(f)
        return ret


class _Buf:
    __slots__ = ("w", "r")
    def __init__(self):
        self.w = None
        self.r = []

    def written(self, sem, val):
        self.w = (sem, val)
        self.r = []

    def read_by(self, sem, val):
        self.r.append((sem, val))

    def ready(self):
        return [self.w] if self.w else []

    def free(self):
        out = list(self.r)
        if self.w:
            out.append(self.w)
        return out


def build_nc():
    nc = bass.Bass("TRN2", target_bir_lowering=False)

    x_d = nc.dram_tensor("x", [S, D], F32, kind="ExternalInput").ap()
    qwT_d = nc.dram_tensor("qwT", [D, D], F32, kind="ExternalInput").ap()
    kwT_d = nc.dram_tensor("kwT", [D, D], F32, kind="ExternalInput").ap()
    vwT_d = nc.dram_tensor("vwT", [D, D], F32, kind="ExternalInput").ap()
    owT_d = nc.dram_tensor("owT", [D, D], F32, kind="ExternalInput").ap()
    qb_d = nc.dram_tensor("qb", [1, D], F32, kind="ExternalInput").ap()
    kb_d = nc.dram_tensor("kb", [1, D], F32, kind="ExternalInput").ap()
    vb_d = nc.dram_tensor("vb", [1, D], F32, kind="ExternalInput").ap()
    ob_d = nc.dram_tensor("ob", [1, D], F32, kind="ExternalInput").ap()
    w1T_d = nc.dram_tensor("w1T", [D, HF], F32, kind="ExternalInput").ap()
    w3T_d = nc.dram_tensor("w3T", [D, HF], F32, kind="ExternalInput").ap()
    w2g_d = nc.dram_tensor("w2g", [HF, D], F32, kind="ExternalInput").ap()
    h64_d = nc.dram_tensor("h64i2", [128, 128], BF16, kind="ExternalInput").ap()
    h128_d = nc.dram_tensor("h128", [128, 128], BF16, kind="ExternalInput").ap()
    id_d = nc.dram_tensor("ident", [128, 128], BF16, kind="ExternalInput").ap()

    okind = "ExternalOutput" if DBG else "Internal"
    x1h = nc.dram_tensor("x1h", [S, D], F32, kind=okind).ap()
    hidh = nc.dram_tensor("hidh", [NT, 128, 64, 128], BF16, kind=okind).ap()
    parkh = nc.dram_tensor("parkh", [S, D], F32, kind=okind).ap()
    out_d = nc.dram_tensor("out", [S, D], F32, kind="ExternalOutput").ap()
    if DBG:
        dbg_ln1T = nc.dram_tensor("dbg_ln1T", [128, NT * S], BF16, kind="ExternalOutput").ap()
        dbg_kvs = nc.dram_tensor("dbg_kvs", [128, HEADS * 129], BF16, kind="ExternalOutput").ap()
        dbg_qpT0 = nc.dram_tensor("dbg_qpT0", [128, S], BF16, kind="ExternalOutput").ap()
        dbg_att = nc.dram_tensor("dbg_att", [128, NT * S], BF16, kind="ExternalOutput").ap()
        dbg_o2 = nc.dram_tensor("dbg_o2", [128, D], F32, kind="ExternalOutput").ap()
        dbg_z = nc.dram_tensor("dbg_z", [128, 48], F32, kind="ExternalOutput").ap()
        dbg_st = nc.dram_tensor("dbg_st", [128, 64], F32, kind="ExternalOutput").ap()
        dbg_ln2T = nc.dram_tensor("dbg_ln2T", [128, NT * S], BF16, kind="ExternalOutput").ap()
        dbg_xe = nc.dram_tensor("dbg_xe", [128, D], F32, kind="ExternalOutput").ap()

    es = ExitStack()
    sbuf = lambda n, s, d: es.enter_context(nc.sbuf_tensor(n, s, d))
    psum = lambda n, s, d: es.enter_context(nc.psum_tensor(n, s, d))
    msem = lambda n: _Sem(es.enter_context(nc.semaphore(n)))

    lnT = sbuf("lnT", [128, NT * S], BF16)
    wb = sbuf("wb", [128, NT * S], BF16)
    ab = sbuf("ab", [128, 8704], BF16)
    kvs = sbuf("kvs", [128, HEADS * 129], BF16)
    xb0 = sbuf("xb0", [128, D], F32)
    xb1 = sbuf("xb1", [128, D], F32)
    big0 = sbuf("big0", [128, D], F32)
    big1 = sbuf("big1", [128, D], F32)
    sm = sbuf("sm", [128, D], F32)
    st32 = sbuf("st32", [128, 96], F32)
    cst = sbuf("cst", [128, 4], F32)
    dbgzt = sbuf("dbgzt", [128, 48], F32) if DBG else None
    ones1 = sbuf("ones1", [1, 128], BF16)
    onesr = sbuf("onesr", [1, 512], BF16)
    onescol = sbuf("onescol", [128, 1], BF16)
    qb1r = sbuf("qb1r", [1, D], BF16)
    kb1r = sbuf("kb1r", [1, D], BF16)
    vbr = sbuf("vbr", [1, D], BF16)
    obr = sbuf("obr", [1, D], BF16)
    h64sb = sbuf("h64sb", [128, 128], BF16)
    h128sb = sbuf("h128sb", [128, 128], BF16)
    idsb = sbuf("idsb", [128, 128], BF16)

    PS = [psum(f"ps{i}", [128, 512], F32) for i in range(6)]
    PT = [psum("pt6", [128, 1024], BF16), psum("pt7", [128, 1024], BF16)]

    SXs = [msem("sxa"), msem("sxb")]
    SPs = [msem("spa"), msem("spb")]
    Gs = [msem("ga"), msem("gb"), msem("gc"), msem("gd")]
    GOW = msem("gow")
    GBI = msem("gbi")
    SCN = msem("scn")
    STs = [msem("sta"), msem("stb")]
    ST = msem("st")
    PEs = msem("pe")
    VEs = msem("ve")
    SEs = msem("se")

    B = _Builder()
    op, wait = B.op, B.wait

    CM1 = cst[:, 0:1]
    CEPS = cst[:, 1:2]
    CMH = cst[:, 2:3]
    CRSQ = cst[:, 3:4]


    def slot_dma(eng, sem, dst, srcap):
        wait(eng, sem, sem.v)
        return op(eng, 'dma_start', dst, srcap, inc=sem, incn=16)

    op('v', 'memset', ones1[:], 1.0)
    op('v', 'memset', onesr[:], 1.0)
    op('v', 'memset', onescol[:], 1.0)
    op('v', 'memset', cst[:, 0:1], -1.0)
    op('v', 'memset', cst[:, 1:2], 1e-5)
    op('v', 'memset', cst[:, 2:3], -0.5)
    init_v = op('v', 'memset', cst[:, 3:4], RSQ, inc=VEs)

    op('g', 'dma_start', qb1r[:], qb_d, inc=GBI, incn=16)
    op('g', 'dma_start', kb1r[:], kb_d, inc=GBI, incn=16)
    op('g', 'dma_start', vbr[:], vb_d, inc=GBI, incn=16)
    bias_g = op('g', 'dma_start', obr[:], ob_d, inc=GBI, incn=16)

    op('y', 'dma_start', h64sb[:], h64_d, inc=SCN, incn=16)
    op('y', 'dma_start', h128sb[:], h128_d, inc=SCN, incn=16)
    consts_y = op('y', 'dma_start', idsb[:], id_d, inc=SCN, incn=16)

    wait('s', GBI, bias_g)
    op('s', 'add', qb1r[:], qb1r[:], 1.0)
    bias_s = op('s', 'add', kb1r[:], kb1r[:], 1.0, inc=SEs)
    wait('t', VEs, init_v)
    wait('s', VEs, init_v)


    xb = [_Buf(), _Buf()]
    psb = [_Buf() for _ in range(6)]
    ptb = [_Buf(), _Buf()]
    lnT_b = _Buf()
    wb_b = _Buf()
    smslot = [_Buf() for _ in range(4)]
    abslot = {}
    big0_b = _Buf(); big1_b = _Buf(); sm_b = _Buf()
    junk_b = _Buf()
    st_par = [_Buf(), _Buf(), _Buf(), _Buf()]

    XBT = [xb0, xb1]


    def ln_stats_part1(src_ap, src_buf, t, pset=None, sjunk=None):
        par = (t % 2) if pset is None else pset
        base = par * 16
        sumx = st32[:, base + 0:base + 1]
        sumq = st32[:, base + 1:base + 2]
        mneg = st32[:, base + 2:base + 3]
        v1 = st32[:, base + 3:base + 4]
        m2 = st32[:, base + 4:base + 5]
        var = st32[:, base + 5:base + 6]
        junk = ab[:, 0:2048]
        B.waits('v', st_par[par].free())
        B.waits('v', src_buf.ready())
        B.waits('v', junk_b.free())
        s1 = op('v', 'scalar_tensor_tensor', junk, src_ap, 0.0, src_ap,
                ALU.add, ALU.mult, accum_out=sumq, inc=VEs)
        junk_b.written(VEs, s1)
        sj_ap, sj_b = (big1[:], big1_b) if sjunk is None else sjunk
        B.waits('s', st_par[par].free())
        B.waits('s', src_buf.ready())
        B.waits('s', sj_b.free())
        ssum = op('s', 'activation', sj_ap, src_ap, AF.Identity,
                  accum_out=sumx, inc=SEs)
        sj_b.written(SEs, ssum)
        wait('v', VEs, s1)
        wait('v', SEs, ssum)
        op('v', 'tensor_scalar', mneg, sumx, -1.0 / D, None, ALU.mult)
        g1 = op('v', 'tensor_scalar', v1, sumq, 1.0 / D, None, ALU.mult,
                inc=VEs)
        wait('v', VEs, g1)
        g2 = op('v', 'tensor_tensor', m2, mneg, mneg, ALU.mult, inc=VEs)
        wait('v', VEs, g2)
        vstat = op('v', 'tensor_tensor', var, v1, m2, ALU.subtract, inc=VEs)
        return par, vstat

    def ln_stats_part2(src_ap, src_buf, par, vstat, out_ap):
        base = par * 16
        mneg = st32[:, base + 2:base + 3]
        var = st32[:, base + 5:base + 6]
        lnv = st32[:, base + 6:base + 7]
        rstd = st32[:, base + 7:base + 8]
        ngm = st32[:, base + 8:base + 9]
        wait('s', VEs, vstat)
        sg0 = op('s', 'activation', lnv, var, AF.Ln, bias=CEPS, inc=SEs)
        wait('s', SEs, sg0)
        sg1 = op('s', 'activation', rstd, lnv, AF.Exp, scale=CMH, inc=SEs)
        wait('s', SEs, sg1)
        sg2 = op('s', 'activation', ngm, mneg, AF.Identity, scale=rstd,
                 inc=SEs)
        wait('s', SEs, sg2)
        sdone = op('s', 'activation', out_ap, src_ap, AF.Identity,
                   bias=ngm, scale=rstd, inc=SEs)
        st_par[par].written(SEs, sdone)
        src_buf.read_by(SEs, sdone)
        return sdone

    def ln_stats_normalize(src_ap, src_buf, t, out_ap, out_dtype_note,
                           extra_s_waits=(), pset=None, sjunk=None):
        par = (t % 2) if pset is None else pset
        base = par * 16
        sumx = st32[:, base + 0:base + 1]
        sumq = st32[:, base + 1:base + 2]
        mneg = st32[:, base + 2:base + 3]
        v1 = st32[:, base + 3:base + 4]
        m2 = st32[:, base + 4:base + 5]
        var = st32[:, base + 5:base + 6]
        lnv = st32[:, base + 6:base + 7]
        rstd = st32[:, base + 7:base + 8]
        ngm = st32[:, base + 8:base + 9]
        junk = ab[:, 0:2048]
        B.waits('v', st_par[par].free())
        B.waits('v', src_buf.ready())
        B.waits('v', junk_b.free())
        s1 = op('v', 'scalar_tensor_tensor', junk, src_ap, 0.0, src_ap,
                ALU.add, ALU.mult, accum_out=sumq, inc=VEs)
        junk_b.written(VEs, s1)
        sj_ap, sj_b = (big1[:], big1_b) if sjunk is None else sjunk
        B.waits('s', st_par[par].free())
        B.waits('s', src_buf.ready())
        B.waits('s', sj_b.free())
        ssum = op('s', 'activation', sj_ap, src_ap, AF.Identity,
                  accum_out=sumx, inc=SEs)
        sj_b.written(SEs, ssum)
        wait('v', VEs, s1)
        wait('v', SEs, ssum)
        op('v', 'tensor_scalar', mneg, sumx, -1.0 / D, None, ALU.mult)
        g1 = op('v', 'tensor_scalar', v1, sumq, 1.0 / D, None, ALU.mult,
                inc=VEs)
        wait('v', VEs, g1)
        g2 = op('v', 'tensor_tensor', m2, mneg, mneg, ALU.mult, inc=VEs)
        wait('v', VEs, g2)
        vstat = op('v', 'tensor_tensor', var, v1, m2, ALU.subtract, inc=VEs)
        if DBG and t in (0, 2) and dbg_phase[0] == 'A':
            off = 0 if t == 0 else 32
            wait('y', VEs, vstat)
            op('y', 'dma_start', dbg_st[:, off:off + 16], st32[:, base:base + 16],
               inc=ST, incn=16)
        wait('s', VEs, vstat)
        for swait in extra_s_waits:
            B.waits('s', [swait])
        sg0 = op('s', 'activation', lnv, var, AF.Ln, bias=CEPS, inc=SEs)
        wait('s', SEs, sg0)
        sg1 = op('s', 'activation', rstd, lnv, AF.Exp, scale=CMH, inc=SEs)
        wait('s', SEs, sg1)
        sg2 = op('s', 'activation', ngm, mneg, AF.Identity, scale=rstd,
                 inc=SEs)
        wait('s', SEs, sg2)
        sdone = op('s', 'activation', out_ap, src_ap, AF.Identity,
                   bias=ngm, scale=rstd, inc=SEs)
        if DBG and t in (0, 2) and dbg_phase[0] == 'A':
            off = 16 if t == 0 else 48
            wait('y', SEs, sdone)
            op('y', 'dma_start', dbg_st[:, off:off + 16], st32[:, base:base + 16],
               inc=ST, incn=16)
        st_par[par].written(SEs, sdone)
        src_buf.read_by(SEs, sdone)
        return sdone

    def transpose_tile_to(dst_cols_fn, src_ap, ln_ready_val, tag):
        for k in range(4):
            bk = (tag + k) % 2
            pt = PT[bk]
            B.waits('t', ptb[bk].free())
            wait('t', SEs, ln_ready_val)
            wait('t', SCN, consts_y)
            for j in range(4):
                dt = 4 * k + j
                lastj = (j == 3)
                pe_v = op('t', 'transpose', pt[:, j * 128:(j + 1) * 128],
                          src_ap[:, dt * 128:(dt + 1) * 128], idsb[:],
                          inc=(PEs if lastj else None))
            ptb[bk].written(PEs, pe_v)
            wait('v', PEs, pe_v)
            ev = op('v', 'tensor_copy', dst_cols_fn(4 * k),
                    pt[:, 0:512], inc=VEs)
            ptb[bk].read_by(VEs, ev)
        return ev

    dbg_phase = ['X']
    xv = x_d.rearrange("(t p) d -> t p d", p=128)
    lnbuf_slots = [ab[:, 2048:4096], ab[:, 4096:6144]]
    lnbuf_b = [_Buf(), _Buf()]
    ln1T_tile_ready = []
    xs4 = [xb0, xb1, big0, sm]
    xs4_b = [xb[0], xb[1], big0_b, sm_b]
    xs4_sem = [SXs[0], SXs[1], SPs[0], SPs[1]]

    st_a = {}

    def ln1_s1(t):
        s4 = t % 3
        B.waits('y', xs4_b[s4].free())
        xw = slot_dma('y', xs4_sem[s4], xs4[s4][:], xv[t])
        xs4_b[s4].written(xs4_sem[s4], xw)
        wait('v', xs4_sem[s4], xw)
        par, vstat = ln_stats_part1(xs4[s4][:], xs4_b[s4], t, pset=t % 3)
        st_a[t] = (par, vstat, s4)

    def ln1_s2(t):
        sl = t % 2
        par, vstat, s4 = st_a[t]
        B.waits('s', lnbuf_b[sl].free())
        lnr = ln_stats_part2(xs4[s4][:], xs4_b[s4], par, vstat,
                             lnbuf_slots[sl][:, :])
        lnbuf_b[sl].written(SEs, lnr)
        st_a[t] = (par, vstat, s4, lnr)

    def ln1_s3(t):
        sl = t % 2
        lnr = st_a[t][3]
        lnTv = lnT[:].rearrange("p (dt s) -> p dt s", dt=NT)
        def dstf(dt0, t=t, lnTv=lnTv):
            return lnTv[:, dt0:dt0 + 4, t * 128:(t + 1) * 128]
        ev = transpose_tile_to(dstf, lnbuf_slots[sl], lnr, tag=t * 4)
        ln1T_tile_ready.append(ev)
        lnbuf_b[sl].read_by(PEs, PEs.v)
    if DBG:
        wait('y', VEs, ln1T_ready)
        dln = op('y', 'dma_start', dbg_ln1T, lnT[:], inc=ST, incn=16)
        lnT_b.read_by(ST, dln)

    dbg_phase[0] = 'B'
    kwv = kwT_d.rearrange("(dt p) o -> p dt o", p=128)
    vwv = vwT_d.rearrange("(dt p) o -> p dt o", p=128)
    wslot_b = [_Buf() for _ in range(4)]
    kp_t = [ab[:, 6144:6656], ab[:, 6656:7168]]
    v_t = [ab[:, 7168:7680], ab[:, 7680:8192]]
    kp_b = [_Buf(), _Buf()]
    v_b = [_Buf(), _Buf()]
    kvs_ready = None

    def proj_kv(oc, t, which):
        if which == 'k':
            bi = t % 2
            wsl = (oc % 2)
            brow = kb1r
        else:
            bi = 2 + t % 2
            wsl = 2 + (oc % 2)
            brow = vbr
        bank = PS[bi]
        B.waits('t', psb[bi].free())
        wait('t', Gs[wsl], wload_val[wsl])
        wait('t', VEs, ln1T_tile_ready[t])
        wait('t', SEs, bias_s)
        for dt in range(NT):
            op('t', 'matmul', bank[:, 0:512],
               lnT[:, dt * S + t * 128: dt * S + (t + 1) * 128],
               wb[:, wsl * 8192 + dt * 512: wsl * 8192 + (dt + 1) * 512],
               start=(dt == 0), stop=False)
        pv = op('t', 'matmul', bank[:, 0:512], ones1[:],
                brow[:, oc * 512:(oc + 1) * 512],
                start=False, stop=True, inc=PEs)
        psb[bi].written(PEs, pv)
        wslot_b[wsl].read_by(PEs, pv)
        return bi, pv

    def evac_kp(oc, t, bi, pv):
        sl = t % 2
        smi = sl
        bank = PS[bi]
        B.waits('s', smslot[smi].free())
        wait('s', PEs, pv)
        ebuf = sm[:, smi * 512:(smi + 1) * 512]
        sv = op('s', 'activation', ebuf, bank[:, 0:512], AF.Exp, bias=CM1,
                inc=SEs)
        smslot[smi].written(SEs, sv)
        B.waits('v', kp_b[sl].free())
        wait('v', SEs, sv)
        vv = op('v', 'scalar_tensor_tensor', kp_t[sl], ebuf, 1.0,
                bank[:, 0:512], ALU.min, ALU.max, inc=VEs)
        kp_b[sl].written(VEs, vv)
        smslot[smi].read_by(VEs, vv)
        psb[bi].read_by(VEs, vv)
        return vv

    def evac_v(oc, t, bi, pv):
        sl = t % 2
        bank = PS[bi]
        B.waits('v', v_b[sl].free())
        wait('v', PEs, pv)
        vv = op('v', 'tensor_copy', v_t[sl], bank[:, 0:512], inc=VEs)
        v_b[sl].written(VEs, vv)
        psb[bi].read_by(VEs, vv)
        return vv

    wload_val = [0, 0, 0, 0]
    for oc in range(4):
        for wsl, view in ((oc % 2, kwv), (2 + oc % 2, vwv)):
            B.waits('g', wslot_b[wsl].free())
            gv = slot_dma('g', Gs[wsl],
                          wb[:, wsl * 8192:(wsl + 1) * 8192]
                          .rearrange("p (dt c) -> p dt c", dt=NT),
                          view[:, :, oc * 512:(oc + 1) * 512])
            wslot_b[wsl].written(Gs[wsl], gv)
            wload_val[wsl] = gv
        d0 = 2 if oc == 0 else 0
        for t in range(NT + 1 + d0):
            if oc == 0:
                if t < NT:
                    ln1_s1(t)
                if 0 <= t - 1 < NT:
                    ln1_s2(t - 1)
                if 0 <= t - 2 < NT:
                    ln1_s3(t - 2)
            tp_ = t - d0
            if 0 <= tp_ < NT:
                bi_k, pv_k = proj_kv(oc, tp_, 'k')
                bi_v, pv_v = proj_kv(oc, tp_, 'v')
                evac_kp(oc, tp_, bi_k, pv_k)
                evac_v(oc, tp_, bi_v, pv_v)
            tt = tp_ - 1
            if tt >= 0:
                sl = tt % 2
                B.waits('t', kp_b[sl].ready())
                B.waits('t', v_b[sl].ready())
                first = (tt == 0)
                if first:
                    B.waits('t', psb[4].free())
                    B.waits('t', psb[5].free())
                last = (tt == NT - 1)
                for hh in range(4):
                    kvbank = PS[4 + hh // 2]
                    co = (hh % 2) * 256
                    st_flag = first and hh == 0
                    st_bank_first = first and (hh % 2 == 0)
                    op('t', 'matmul', kvbank[:, co:co + 128],
                       kp_t[sl][:, hh * 128:(hh + 1) * 128],
                       v_t[sl][:, hh * 128:(hh + 1) * 128],
                       start=st_bank_first, stop=last,
                       skip_group_check=True)
                    lasth = (hh == 3)
                    pv = op('t', 'matmul', kvbank[:, co + 128:co + 129],
                            kp_t[sl][:, hh * 128:(hh + 1) * 128],
                            onescol[:], start=False, stop=last,
                            skip_group_check=True,
                            inc=(PEs if lasth else None))
                kp_b[sl].read_by(PEs, pv)
                v_b[sl].read_by(PEs, pv)
                if last:
                    kvdone = pv
        wait('v', PEs, kvdone)
        for hh in range(4):
            h = oc * 4 + hh
            kvbank = PS[4 + hh // 2]
            co = (hh % 2) * 256
            vv = op('v', 'tensor_copy', kvs[:, h * 129:(h + 1) * 129],
                    kvbank[:, co:co + 129], inc=VEs)
            psb[4 + hh // 2].read_by(VEs, vv)
        kvs_ready = VEs.v
        if oc == 0:
            ln1T_ready = VEs.v
            lnT_b.written(VEs, ln1T_tile_ready[-1])

    if DBG:
        wait('y', VEs, kvs_ready)
        op('y', 'dma_start', dbg_kvs, kvs[:], inc=ST, incn=16)
    qwv = qwT_d.rearrange("(dt p) o -> p dt o", p=128)
    qsl = [ab[:, 0:2048], ab[:, 2048:4096]]
    qsl_b = [_Buf(), _Buf()]
    qpT = [ab[:, 4096:6144], ab[:, 6144:8192]]
    qpT_b = [_Buf(), _Buf()]
    att_t = [ab[:, 8192:8704], ab[:, 8192:8704]]
    _ab1 = _Buf()
    att_b = [_ab1, _ab1]
    rz_b = _Buf()
    wb_pe_done = PEs.v
    wb_b.written(PEs, wb_pe_done)

    def emit_qw_load(h):
        sl = h % 2
        B.waits('g', qsl_b[sl].free())
        wait('g', PEs, wb_pe_done)
        wait('g', VEs, ln1T_ready)
        gv = slot_dma('g', Gs[sl],
                      qsl[sl].rearrange("p (dt c) -> p dt c", dt=NT),
                      qwv[:, :, h * 128:(h + 1) * 128])
        qsl_b[sl].written(Gs[sl], gv)
        return gv

    def emit_qp_chunk(h, gv, sc):
        sl = h % 2
        bi = sc % 3
        bank = PS[bi]
        B.waits('t', psb[bi].free())
        wait('t', Gs[sl], gv)
        for dt in range(NT):
            op('t', 'matmul', bank[:, 0:512],
               qsl[sl][:, dt * 128:(dt + 1) * 128],
               lnT[:, dt * S + sc * 512: dt * S + (sc + 1) * 512],
               start=(dt == 0), stop=False)
        pv = op('t', 'matmul', bank[:, 0:512],
                qb1r[:, h * 128:(h + 1) * 128], onesr[:],
                start=False, stop=True, inc=PEs)
        psb[bi].written(PEs, pv)
        qsl_b[sl].read_by(PEs, pv)
        smi = sc % 2
        B.waits('s', smslot[smi].free())
        wait('s', PEs, pv)
        ebuf = sm[:, smi * 512:(smi + 1) * 512]
        sv = op('s', 'activation', ebuf, bank[:, 0:512], AF.Exp, bias=CM1,
                inc=SEs)
        smslot[smi].written(SEs, sv)
        if sc == 0:
            B.waits('v', qpT_b[sl].free())
        wait('v', SEs, sv)
        vv = op('v', 'scalar_tensor_tensor',
                qpT[sl][:, sc * 512:(sc + 1) * 512], ebuf, 1.0,
                bank[:, 0:512], ALU.min, ALU.max, inc=VEs)
        smslot[smi].read_by(VEs, vv)
        psb[bi].read_by(VEs, vv)
        if sc == 3:
            qpT_b[sl].written(VEs, vv)
        return vv

    def emit_att_z(h, qp_val):
        sl = h % 2
        B.waits('t', psb[5].free())
        wait('t', VEs, qp_val)
        wait('t', VEs, kvs_ready)
        for t2 in range(NT):
            pv = op('t', 'matmul', PS[5][:, t2:t2 + 1],
                    qpT[sl][:, t2 * 128:(t2 + 1) * 128],
                    kvs[:, h * 129 + 128: h * 129 + 129],
                    start=(t2 == 0), stop=(t2 == NT - 1),
                    skip_group_check=True,
                    inc=(PEs if t2 == NT - 1 else None))
        psb[5].written(PEs, pv)
        B.waits('v', rz_b.free())
        wait('v', PEs, pv)
        tsv = op('v', 'tensor_scalar', st32[:, 32:48], PS[5][:, 0:16], 1e-6,
                 None, ALU.add, inc=VEs)
        wait('v', VEs, tsv)
        rzv = op('v', 'reciprocal', st32[:, 48:64], st32[:, 32:48], inc=VEs)
        rz_b.written(VEs, rzv)
        psb[5].read_by(VEs, rzv)
        return rzv

    def emit_att_group(h, g, rzv):
        sl = h % 2
        bi = 3 + g % 2
        bank = PS[bi]
        B.waits('t', psb[bi].free())
        for j in range(4):
            t2 = 4 * g + j
            pv = op('t', 'matmul', bank[:, j * 128:(j + 1) * 128],
                    qpT[sl][:, t2 * 128:(t2 + 1) * 128],
                    kvs[:, h * 129: h * 129 + 128],
                    start=True, stop=True, skip_group_check=True,
                    inc=(PEs if j == 3 else None))
        psb[bi].written(PEs, pv)
        asl = g % 2
        B.waits('s', att_b[asl].free())
        wait('s', PEs, pv)
        wait('s', VEs, rzv)
        for j in range(4):
            t2 = 4 * g + j
            sv = op('s', 'activation', att_t[asl][:, j * 128:(j + 1) * 128],
                    bank[:, j * 128:(j + 1) * 128], AF.Identity,
                    scale=st32[:, 48 + t2:49 + t2],
                    inc=(SEs if j == 3 else None))
        att_b[asl].written(SEs, sv)
        psb[bi].read_by(SEs, sv)
        rz_b.read_by(SEs, sv)
        bk = g % 2
        B.waits('t', ptb[bk].free())
        wait('t', SEs, sv)
        for j in range(4):
            tv = op('t', 'transpose', PT[bk][:, j * 128:(j + 1) * 128],
                    att_t[asl][:, j * 128:(j + 1) * 128], idsb[:],
                    inc=(PEs if j == 3 else None))
        ptb[bk].written(PEs, tv)
        att_b[asl].read_by(PEs, tv)
        wait('v', PEs, tv)
        B.waits('v', wb_b.ready())
        ev = op('v', 'tensor_copy',
                wb[:, h * S + g * 512: h * S + (g + 1) * 512],
                PT[bk][:, 0:512], inc=VEs)
        ptb[bk].read_by(VEs, ev)
        return ev

    prev = None
    for h in range(HEADS):
        gv = emit_qw_load(h)
        for sc in range(4):
            qv = emit_qp_chunk(h, gv, sc)
            if prev is not None:
                ph, pqv = prev
                if sc == 1:
                    rz_prev = emit_att_z(ph, pqv)
                elif sc == 2:
                    emit_att_group(ph, 0, rz_prev)
                    emit_att_group(ph, 1, rz_prev)
                elif sc == 3:
                    emit_att_group(ph, 2, rz_prev)
                    ev = emit_att_group(ph, 3, rz_prev)
                    qpT_b[ph % 2].read_by(PEs, PEs.v)
        prev = (h, qv)
    ph, pqv = prev
    rz_prev = emit_att_z(ph, pqv)
    for g in range(4):
        ev = emit_att_group(ph, g, rz_prev)
    qpT_b[ph % 2].read_by(PEs, PEs.v)
    attT_ready = ev
    if DBG:
        wait('y', VEs, attT_ready)
        da = op('y', 'dma_start', dbg_att, wb[:], inc=ST, incn=16)
        wb_b.read_by(ST, da)

    lnT_pe_done = PEs.v
    owv = owT_d.rearrange("(ot p) d -> p ot d", p=128)
    lnTv2 = lnT[:].rearrange("p (ot d) -> p ot d", ot=NT)
    ow_vals = []
    for dc in range(4):
        wait('g', PEs, lnT_pe_done)
        wait('g', GOW, GOW.v)
        gv = op('g', 'dma_start', lnTv2[:, :, dc * 512:(dc + 1) * 512],
                owv[:, :, dc * 512:(dc + 1) * 512], inc=GOW, incn=16)
        ow_vals.append(gv)
    ow_ready = ow_vals[-1]

    x1v = x1h.rearrange("(t p) d -> t p d", p=128)
    x1st_val = [None] * NT
    ln2_park = [None] * NT
    o2slots = [big0, sm]
    o2_b = [big0_b, sm_b]
    for t in range(NT):
        sl = t % 2
        B.waits('y', xb[sl].free())
        xw = slot_dma('y', SXs[sl], XBT[sl][:], xv[t])
        xb[sl].written(SXs[sl], xw)
        o2 = o2slots[sl]
        o2b = o2_b[sl]
        for dc in range(4):
            bi = dc % 3
            bank = PS[bi]
            B.waits('t', psb[bi].free())
            wait('t', GOW, ow_vals[dc])
            wait('t', VEs, attT_ready)
            for ot in range(NT):
                op('t', 'matmul', bank[:, 0:512],
                   wb[:, ot * S + t * 128: ot * S + (t + 1) * 128],
                   lnT[:, ot * S + dc * 512: ot * S + (dc + 1) * 512],
                   start=(ot == 0), stop=False)
            pv = op('t', 'matmul', bank[:, 0:512], ones1[:],
                    obr[:, dc * 512:(dc + 1) * 512], start=False, stop=True,
                    inc=PEs)
            psb[bi].written(PEs, pv)
            B.waits('v', o2b.free() if dc == 0 else [])
            wait('v', PEs, pv)
            vv = op('v', 'tensor_copy', o2[:, dc * 512:(dc + 1) * 512],
                    bank[:, 0:512], inc=VEs)
            psb[bi].read_by(VEs, vv)
        o2b.written(VEs, vv)
        if DBG and t == 0:
            wait('y', VEs, vv)
            do2 = op('y', 'dma_start', dbg_o2, o2[:], inc=ST, incn=16)
            o2b.read_by(ST, do2)
        B.waits('s', big1_b.free())
        lnr = ln_stats_normalize(o2[:], o2b, t, big1[:], "f32")
        big1_b.written(SEs, lnr)
        wait('v', SEs, lnr)
        wait('v', SXs[sl], xw)
        av = op('v', 'tensor_tensor', o2[:], big1[:], XBT[sl][:], ALU.add,
                inc=VEs)
        o2b.written(VEs, av)
        big1_b.read_by(VEs, av)
        xb[sl].read_by(VEs, av)
        wait('g', VEs, av)
        stv = slot_dma('g', STs[sl], x1v[t], o2[:])
        o2b.read_by(STs[sl], stv)
        x1st_val[t] = stv
        par2, vstat2 = ln_stats_part1(o2[:], o2b, t, pset=2 + t % 2,
                                      sjunk=(XBT[sl][:], xb[sl]))
        base2 = par2 * 16
        rstd_p = st32[:, 64 + 2 * t:65 + 2 * t]
        ngm_p = st32[:, 65 + 2 * t:66 + 2 * t]
        wait('s', VEs, vstat2)
        pg0 = op('s', 'activation', st32[:, base2 + 6:base2 + 7],
                 st32[:, base2 + 5:base2 + 6], AF.Ln, bias=CEPS, inc=SEs)
        wait('s', SEs, pg0)
        pg1 = op('s', 'activation', rstd_p, st32[:, base2 + 6:base2 + 7],
                 AF.Exp, scale=CMH, inc=SEs)
        wait('s', SEs, pg1)
        pg2 = op('s', 'activation', ngm_p, st32[:, base2 + 2:base2 + 3],
                 AF.Identity, scale=rstd_p, inc=SEs)
        st_par[par2].written(SEs, pg2)
        o2b.read_by(SEs, pg2)
        ln2_park[t] = (pg2, rstd_p, ngm_p)

    lnT_pe_done2 = PEs.v
    wb_pe_done2 = PEs.v
    ln2T_tile_ready = []
    dbg_phase[0] = 'A'
    lnTv = lnT[:].rearrange("p (dt s) -> p dt s", dt=NT)
    st_e = {}
    for i in range(NT + 1):
        if i < NT:
            t = i
            sl = t % 2
            s4 = t % 4
            B.waits('y', xs4_b[s4].free())
            wait('y', STs[sl], x1st_val[t])
            xw = slot_dma('y', xs4_sem[s4], xs4[s4][:], x1v[t])
            xs4_b[s4].written(xs4_sem[s4], xw)
            pgv, rstd_p, ngm_p = ln2_park[t]
            B.waits('s', lnbuf_b[sl].free())
            wait('s', xs4_sem[s4], xw)
            wait('s', SEs, pgv)
            lnr = op('s', 'activation', lnbuf_slots[sl][:, :], xs4[s4][:],
                     AF.Identity, bias=ngm_p, scale=rstd_p, inc=SEs)
            lnbuf_b[sl].written(SEs, lnr)
            xs4_b[s4].read_by(SEs, lnr)
            st_e[t] = lnr
        if 0 <= i - 1 < NT:
            t = i - 1
            sl = t % 2
            lnr = st_e[t]
            def dstf2(dt0, t=t, lnTv=lnTv):
                return lnTv[:, dt0:dt0 + 4, t * 128:(t + 1) * 128]
            wait('v', PEs, lnT_pe_done2)
            ev = transpose_tile_to(dstf2, lnbuf_slots[sl], lnr, tag=t * 4)
            ln2T_tile_ready.append(ev)
            lnbuf_b[sl].read_by(PEs, PEs.v)
    ln2T_ready = VEs.v
    dbg_phase[0] = 'X'
    if DBG:
        wait('y', VEs, ln2T_ready)
        dl2 = op('y', 'dma_start', dbg_ln2T, lnT[:], inc=ST, incn=16)
        lnT_b.written(VEs, ln2T_ready)
        lnT_b.read_by(ST, dl2)

    w1v = w1T_d.rearrange("(dt p) h -> p dt h", p=128)
    w3v = w3T_d.rearrange("(dt p) h -> p dt h", p=128)
    def wbu(u, a=0, b=2048):
        return wb[:, u * 2048 + a: u * 2048 + b]
    wfb = {k: _Buf() for k in ['w10', 'w11', 'w30', 'w31', 'hb', 'hg0', 'hg1']}
    wait('s', STs[0], STs[0].v)
    wait('s', STs[1], STs[1].v)
    for m in range(64):
        s1 = m % 2
        w1slot, w1b = (0 if s1 == 0 else 1), wfb['w1' + str(s1)]
        w3slot, w3b = (2 if s1 == 0 else 3), wfb['w3' + str(s1)]
        B.waits('g', w1b.free())
        wait('g', PEs, wb_pe_done2)
        g1 = slot_dma('g', Gs[w1slot],
                      wbu(w1slot).rearrange("p (dt c) -> p dt c", dt=NT),
                      w1v[:, :, m * 128:(m + 1) * 128])
        w1b.written(Gs[w1slot], g1)
        B.waits('g', w3b.free())
        g3 = slot_dma('g', Gs[w3slot],
                      wbu(w3slot).rearrange("p (dt c) -> p dt c", dt=NT),
                      w3v[:, :, m * 128:(m + 1) * 128])
        w3b.written(Gs[w3slot], g3)
        hgslot = 5 + m % 2
        hgb = wfb['hg' + str(m % 2)]
        pend_h64 = None
        for sc in range(4):
            big = sc % 2
            bg, bl = PS[big], PS[2 + big]
            B.waits('t', psb[big].free())
            wait('t', Gs[w1slot], g1)
            wait('t', VEs, ln2T_tile_ready[4 * sc + 3])
            for dt in range(NT):
                op('t', 'matmul', bg[:, 0:512],
                   wbu(w1slot, dt * 128, (dt + 1) * 128),
                   lnT[:, dt * S + sc * 512: dt * S + (sc + 1) * 512],
                   start=(dt == 0), stop=(dt == NT - 1),
                   inc=(PEs if dt == NT - 1 else None))
            pg = PEs.v
            psb[big].written(PEs, pg)
            w1b.read_by(PEs, pg)
            B.waits('t', psb[2 + big].free())
            wait('t', Gs[w3slot], g3)
            for dt in range(NT):
                op('t', 'matmul', bl[:, 0:512],
                   wbu(w3slot, dt * 128, (dt + 1) * 128),
                   lnT[:, dt * S + sc * 512: dt * S + (sc + 1) * 512],
                   start=(dt == 0), stop=(dt == NT - 1),
                   inc=(PEs if dt == NT - 1 else None))
            pl = PEs.v
            psb[2 + big].written(PEs, pl)
            w3b.read_by(PEs, pl)
            if pend_h64 is not None:
                pend_h64()
                pend_h64 = None
            smi = 2 + sc % 2
            B.waits('s', smslot[smi].free())
            wait('s', PEs, pg)
            sbuf_ = sm[:, smi * 512:(smi + 1) * 512]
            sv = op('s', 'activation', sbuf_, bg[:, 0:512], AF.Silu, inc=SEs)
            smslot[smi].written(SEs, sv)
            psb[big].read_by(SEs, sv)
            B.waits('v', wfb['hb'].free() if sc == 0 else [])
            wait('v', SEs, sv)
            wait('v', PEs, pl)
            hv = op('v', 'tensor_tensor', wbu(4, sc * 512, (sc + 1) * 512),
                    sbuf_, bl[:, 0:512], ALU.mult, inc=VEs)
            smslot[smi].read_by(VEs, hv)
            psb[2 + big].read_by(VEs, hv)
            def mk_h64(sc=sc, hv=hv, hgslot=hgslot, hgb=hgb, m=m):
                bi6 = 4 + sc % 2
                bank = PS[bi6]
                B.waits('t', psb[bi6].free())
                wait('t', VEs, hv)
                wait('t', SCN, consts_y)
                pv = op('t', 'matmul', bank[:, 0:512], h64sb[:],
                        wbu(4, sc * 512, (sc + 1) * 512),
                        start=True, stop=True, inc=PEs)
                psb[bi6].written(PEs, pv)
                if sc == 0:
                    B.waits('v', hgb.free())
                wait('v', PEs, pv)
                ev = op('v', 'tensor_copy',
                        wbu(hgslot, sc * 512, (sc + 1) * 512),
                        bank[:, 0:512], inc=VEs)
                psb[bi6].read_by(VEs, ev)
                if sc == 3:
                    hgb.written(VEs, ev)
                    wait('y', VEs, ev)
                    dst = hidh[:, 2 * m:2 * m + 2, :, :].rearrange(
                        "t a b s -> (a b) t s")
                    stv = slot_dma('y', STs[m % 2], dst,
                                   wbu(hgslot).rearrange("p (t s) -> p t s",
                                                         t=NT))
                    hgb.read_by(STs[m % 2], stv)
            pend_h64 = mk_h64
        pend_h64()
        wfb['hb'].written(VEs, VEs.v)
        wfb['hb'].read_by(PEs, PEs.v)

    w2v = w2g_d.rearrange("(b p) d -> p b d", p=128)
    parkv = parkh.rearrange("(t p) d -> t p d", p=128)
    wraw = [ab[:, 0:4096], ab[:, 4096:8192]]
    wraw_b = [_Buf(), _Buf()]
    stripe = [lnT[:, 0:8192], lnT[:, 8192:16384]]
    stripe_b = [_Buf(), _Buf()]
    park_sl = [sm[:, 0:512], sm[:, 512:1024]]
    park_b = [smslot[0], smslot[1]]
    lnT_pe_done3 = PEs.v
    wb_pe_done3 = PEs.v
    w2q_ready = 0
    wb_part_free = {}
    outv = out_d.rearrange("(t p) d -> t p d", p=128)
    fslots = [big0, sm]
    fslots_b = [big0_b, sm_b]
    hidst_done = (STs[0].v, STs[1].v)
    wait('s', STs[0], hidst_done[0])
    wait('s', STs[1], hidst_done[1])
    parkst = [[None] * NT for _ in range(3)]
    GM = [msem("gm0"), msem("gm1")]
    for q in range(4):
        wb_free_val = PEs.v
        wb_part_free_prev = dict(wb_part_free)
        wb_part_free = {}
        if q == 3:
            park_q2_done = (STs[0].v, STs[1].v)
            pend_store = []
        bb_ready = [0] * 8
        for bb in range(8):
            sl = bb % 2
            B.waits('g', wraw_b[sl].free())
            wait('g', VEs, ln2T_ready)
            gv = slot_dma('g', Gs[sl],
                          wraw[sl].rearrange("p (j c) -> p j c", j=8),
                          w2v[:, bb * 8:(bb + 1) * 8, q * 512:(q + 1) * 512])
            wraw_b[sl].written(Gs[sl], gv)
            for j in range(8):
                b = bb * 8 + j
                bi = j % 2
                bank = PS[bi]
                B.waits('t', psb[bi].free())
                wait('t', Gs[sl], gv)
                wait('t', SCN, consts_y)
                pv = op('t', 'matmul', bank[:, 0:512], h128sb[:],
                        wraw[sl][:, j * 512:(j + 1) * 512],
                        start=True, stop=True, inc=PEs)
                psb[bi].written(PEs, pv)
                wraw_b[sl].read_by(PEs, pv)
                wait('s', PEs, pv)
                part = min(b // 16, 3)
                gate = wb_part_free_prev.get(part + 1, wb_free_val)
                wait('s', PEs, max(wb_pe_done3, gate if part < 3 else wb_free_val))
                wait('s', VEs, init_v)
                sv = op('s', 'activation', wb[:, b * 512:(b + 1) * 512],
                        bank[:, 0:512], AF.Identity, scale=CRSQ, inc=SEs)
                psb[bi].read_by(SEs, sv)
            bb_ready[bb] = sv
        w2q_ready = sv
        for t in range(NT):
            sl = t % 2
            B.waits('y', stripe_b[sl].free())
            wait('y', STs[0], hidst_done[0])
            wait('y', STs[1], hidst_done[1])
            wait('y', PEs, lnT_pe_done3)
            lw = slot_dma('y', SPs[sl], stripe[sl], hidh[t].rearrange("a b s -> a (b s)"))
            stripe_b[sl].written(SPs[sl], lw)
            bi = 2 + t % 2
            bank = PS[bi]
            B.waits('t', psb[bi].free())
            wait('t', SPs[sl], lw)
            wait('t', PEs, lnT_pe_done3)
            last_t = (t == NT - 1)
            for b in range(64):
                if b % 8 == 0:
                    wait('t', SEs, bb_ready[b // 8])
                mark = last_t and (b in (15, 31, 47))
                op('t', 'matmul', bank[:, 0:512],
                   stripe[sl][:, b * 128:(b + 1) * 128],
                   wb[:, b * 512:(b + 1) * 512],
                   start=(b == 0), stop=(b == 63),
                   inc=(PEs if (b == 63 or mark) else None))
                if mark:
                    wb_part_free[(b + 1) // 16] = PEs.v
            pv = PEs.v
            psb[bi].written(PEs, pv)
            stripe_b[sl].read_by(PEs, pv)
            if q < 3:
                psl = t % 2
                B.waits('v', park_b[psl].free())
                wait('v', PEs, pv)
                ev = op('v', 'tensor_copy', park_sl[psl], bank[:, 0:512],
                        inc=VEs)
                psb[bi].read_by(VEs, ev)
                park_b[psl].written(VEs, ev)
                wait('g', VEs, ev)
                stv = slot_dma('g', STs[psl],
                               parkv[t][:, q * 512:(q + 1) * 512],
                               park_sl[psl])
                park_b[psl].read_by(STs[psl], stv)
                parkst[q][t] = stv
            else:
                fsl = t % 2
                fb = fslots_b[fsl]
                B.waits('g', fb.free())
                wait('g', STs[0], park_q2_done[0])
                wait('g', STs[1], park_q2_done[1])
                pw = slot_dma('g', GM[fsl], fslots[fsl][:, 0:1536],
                              parkv[t][:, 0:1536])
                B.waits('g', xb[fsl].free())
                xw = slot_dma('g', SXs[fsl], XBT[fsl][:], x1v[t])
                xb[fsl].written(SXs[fsl], xw)
                wait('v', GM[fsl], pw)
                wait('v', PEs, pv)
                ev = op('v', 'tensor_copy', fslots[fsl][:, 1536:2048],
                        bank[:, 0:512], inc=VEs)
                psb[bi].read_by(VEs, ev)
                fb.written(VEs, ev)
                B.waits('s', big1_b.free())
                wait('s', GM[fsl], pw)
                lnr = ln_stats_normalize(fslots[fsl][:], fb, t, big1[:],
                                         "f32")
                big1_b.written(SEs, lnr)
                wait('v', SEs, lnr)
                wait('v', SXs[fsl], xw)
                av = op('v', 'tensor_tensor', fslots[fsl][:], big1[:],
                        XBT[fsl][:], ALU.add, inc=VEs)
                fb.written(VEs, av)
                big1_b.read_by(VEs, av)
                xb[fsl].read_by(VEs, av)
                pend_store.append((t, fsl, av))
                if len(pend_store) >= 2:
                    t_, fsl_, av_ = pend_store.pop(0)
                    wait('g', VEs, av_)
                    stv = slot_dma('g', ST, outv[t_], fslots[fsl_][:])
                    fslots_b[fsl_].read_by(ST, stv)
    for t_, fsl_, av_ in pend_store:
        wait('g', VEs, av_)
        stv = slot_dma('g', ST, outv[t_], fslots[fsl_][:])
        fslots_b[fsl_].read_by(ST, stv)
    wait('y', ST, ST.v)
    wait('y', STs[0], STs[0].v)
    wait('y', STs[1], STs[1].v)

    with nc.Block() as block:
        @block.tensor
        def _(e):
            for f in B.prog['t']:
                f(e)

        @block.vector
        def _(e):
            for f in B.prog['v']:
                f(e)

        @block.scalar
        def _(e):
            for f in B.prog['s']:
                f(e)

        @block.sync
        def _(e):
            for f in B.prog['y']:
                f(e)

        @block.gpsimd
        def _(e):
            for f in B.prog['g']:
                f(e)
    return nc


def _prep_inputs(L0_attn_hidden_states, L0_attn_q_w, L0_attn_q_b, L0_attn_k_w,
                 L0_attn_k_b, L0_attn_v_w, L0_attn_v_b, L0_attn_o_w,
                 L0_attn_o_b, L0_ffn_W1, L0_ffn_W2, L0_ffn_W3):
    c = np.ascontiguousarray
    f = np.float32
    qwT = c(np.asarray(L0_attn_q_w, f).T)
    kwT = c(np.asarray(L0_attn_k_w, f).T)
    vwT = c(np.asarray(L0_attn_v_w, f).T)
    owT = c(np.asarray(L0_attn_o_w, f).T)
    w1T = c(np.asarray(L0_ffn_W1, f).T)
    w3T = c(np.asarray(L0_ffn_W3, f).T)
    w2g = c(np.asarray(L0_ffn_W2, f).T.reshape(128, 64, D).transpose(1, 0, 2)
            .reshape(HF, D))
    qb = c(np.asarray(L0_attn_q_b, f).reshape(1, D))
    kb = c(np.asarray(L0_attn_k_b, f).reshape(1, D))
    vb = c(np.asarray(L0_attn_v_b, f).reshape(1, D))
    ob = c(np.asarray(L0_attn_o_b, f).reshape(1, D))
    h64 = _had(64)
    h64i2 = np.kron(np.eye(2, dtype=np.float32), h64).astype(ml_dtypes.bfloat16)
    h128 = _had(128).astype(ml_dtypes.bfloat16)
    ident = np.eye(128, dtype=ml_dtypes.bfloat16)
    shared = dict(qwT=qwT, kwT=kwT, vwT=vwT, owT=owT, qb=qb, kb=kb, vb=vb,
                  ob=ob, w1T=w1T, w3T=w3T, w2g=w2g, h64i2=h64i2, h128=h128,
                  ident=ident)
    xs = np.asarray(L0_attn_hidden_states, f)
    in_maps = []
    for core in range(8):
        m = dict(shared)
        m["x"] = c(xs[core])
        in_maps.append(m)
    return in_maps


def kernel(**inputs):
    in_maps = _prep_inputs(**inputs)
    nc = build_nc()
    res = run_bass_kernel_spmd(nc, in_maps, core_ids=list(range(8)))
    out = np.stack([r["out"] for r in res.results], axis=0)
    return out.astype(np.float32)


# version 25 (final)
# speedup vs baseline: 1.0360x; 1.0360x over previous
import numpy as np
import ml_dtypes
import concourse.bass as bass
import concourse.mybir as mybir
from concourse.bass_utils import run_bass_kernel_spmd
from contextlib import ExitStack

F32 = mybir.dt.float32
BF16 = mybir.dt.bfloat16
AF = mybir.ActivationFunctionType
ALU = mybir.AluOpType

S = 2048
D = 2048
HEADS = 16
HD = 128
HF = 8192
NT = 16
RSQ = float(np.float32(1.0 / np.sqrt(np.float64(HF))))

DBG = False


def _had(n):
    h = np.array([[1.0]], dtype=np.float32)
    while h.shape[0] < n:
        h = np.block([[h, h], [h, -h]])
    return h


class _Sem:
    __slots__ = ("h", "v")
    def __init__(self, h):
        self.h = h
        self.v = 0


class _Builder:
    def __init__(self):
        self.prog = {k: [] for k in "tvsyg"}
        self._waited = {k: {} for k in "tvsyg"}

    def raw(self, eng, fn):
        self.prog[eng].append(fn)

    def wait(self, eng, sem, val):
        if val is None or val <= 0:
            return
        key = id(sem.h)
        if self._waited[eng].get(key, -1) >= val:
            return
        self._waited[eng][key] = val
        self.prog[eng].append(lambda e, s=sem.h, v=val: e.wait_ge(s, v))

    def waits(self, eng, pairs):
        for sem, val in pairs:
            self.wait(eng, sem, val)

    def op(self, eng, meth, *args, inc=None, incn=1, **kw):
        tgt = None
        ret = None
        if inc is not None:
            inc.v += incn
            tgt = (inc.h, incn)
            ret = inc.v
        def f(e, meth=meth, args=args, kw=kw, tgt=tgt):
            r = getattr(e, meth)(*args, **kw)
            if tgt is not None:
                r.then_inc(*tgt)
        self.prog[eng].append(f)
        return ret


class _Buf:
    __slots__ = ("w", "r")
    def __init__(self):
        self.w = None
        self.r = []

    def written(self, sem, val):
        self.w = (sem, val)
        self.r = []

    def read_by(self, sem, val):
        self.r.append((sem, val))

    def ready(self):
        return [self.w] if self.w else []

    def free(self):
        out = list(self.r)
        if self.w:
            out.append(self.w)
        return out


def build_nc():
    nc = bass.Bass("TRN2", target_bir_lowering=False)

    x_d = nc.dram_tensor("x", [S, D], F32, kind="ExternalInput").ap()
    qwT_d = nc.dram_tensor("qwT", [D, D], F32, kind="ExternalInput").ap()
    kwT_d = nc.dram_tensor("kwT", [D, D], F32, kind="ExternalInput").ap()
    vwT_d = nc.dram_tensor("vwT", [D, D], F32, kind="ExternalInput").ap()
    owT_d = nc.dram_tensor("owT", [D, D], F32, kind="ExternalInput").ap()
    qb_d = nc.dram_tensor("qb", [1, D], F32, kind="ExternalInput").ap()
    kb_d = nc.dram_tensor("kb", [1, D], F32, kind="ExternalInput").ap()
    vb_d = nc.dram_tensor("vb", [1, D], F32, kind="ExternalInput").ap()
    ob_d = nc.dram_tensor("ob", [1, D], F32, kind="ExternalInput").ap()
    w1T_d = nc.dram_tensor("w1T", [D, HF], F32, kind="ExternalInput").ap()
    w3T_d = nc.dram_tensor("w3T", [D, HF], F32, kind="ExternalInput").ap()
    w2g_d = nc.dram_tensor("w2g", [HF, D], F32, kind="ExternalInput").ap()
    h64_d = nc.dram_tensor("h64i2", [128, 128], BF16, kind="ExternalInput").ap()
    h128_d = nc.dram_tensor("h128", [128, 128], BF16, kind="ExternalInput").ap()
    id_d = nc.dram_tensor("ident", [128, 128], BF16, kind="ExternalInput").ap()

    okind = "ExternalOutput" if DBG else "Internal"
    x1h = nc.dram_tensor("x1h", [S, D], F32, kind=okind).ap()
    hidh = nc.dram_tensor("hidh", [NT, 128, 64, 128], BF16, kind=okind).ap()
    parkh = nc.dram_tensor("parkh", [S, D], F32, kind=okind).ap()
    out_d = nc.dram_tensor("out", [S, D], F32, kind="ExternalOutput").ap()
    if DBG:
        dbg_ln1T = nc.dram_tensor("dbg_ln1T", [128, NT * S], BF16, kind="ExternalOutput").ap()
        dbg_kvs = nc.dram_tensor("dbg_kvs", [128, HEADS * 129], BF16, kind="ExternalOutput").ap()
        dbg_qpT0 = nc.dram_tensor("dbg_qpT0", [128, S], BF16, kind="ExternalOutput").ap()
        dbg_att = nc.dram_tensor("dbg_att", [128, NT * S], BF16, kind="ExternalOutput").ap()
        dbg_o2 = nc.dram_tensor("dbg_o2", [128, D], F32, kind="ExternalOutput").ap()
        dbg_z = nc.dram_tensor("dbg_z", [128, 48], F32, kind="ExternalOutput").ap()
        dbg_st = nc.dram_tensor("dbg_st", [128, 64], F32, kind="ExternalOutput").ap()
        dbg_ln2T = nc.dram_tensor("dbg_ln2T", [128, NT * S], BF16, kind="ExternalOutput").ap()
        dbg_xe = nc.dram_tensor("dbg_xe", [128, D], F32, kind="ExternalOutput").ap()

    es = ExitStack()
    sbuf = lambda n, s, d: es.enter_context(nc.sbuf_tensor(n, s, d))
    psum = lambda n, s, d: es.enter_context(nc.psum_tensor(n, s, d))
    msem = lambda n: _Sem(es.enter_context(nc.semaphore(n)))

    lnT = sbuf("lnT", [128, NT * S], BF16)
    wb = sbuf("wb", [128, NT * S], BF16)
    ab = sbuf("ab", [128, 8704], BF16)
    kvs = sbuf("kvs", [128, HEADS * 129], BF16)
    xb0 = sbuf("xb0", [128, D], F32)
    xb1 = sbuf("xb1", [128, D], F32)
    big0 = sbuf("big0", [128, D], F32)
    big1 = sbuf("big1", [128, D], F32)
    sm = sbuf("sm", [128, D], F32)
    st32 = sbuf("st32", [128, 96], F32)
    cst = sbuf("cst", [128, 4], F32)
    dbgzt = sbuf("dbgzt", [128, 48], F32) if DBG else None
    ones1 = sbuf("ones1", [1, 128], BF16)
    onesr = sbuf("onesr", [1, 512], BF16)
    onescol = sbuf("onescol", [128, 1], BF16)
    qb1r = sbuf("qb1r", [1, D], BF16)
    kb1r = sbuf("kb1r", [1, D], BF16)
    vbr = sbuf("vbr", [1, D], BF16)
    obr = sbuf("obr", [1, D], BF16)
    h64sb = sbuf("h64sb", [128, 128], BF16)
    h128sb = sbuf("h128sb", [128, 128], BF16)
    idsb = sbuf("idsb", [128, 128], BF16)

    PS = [psum(f"ps{i}", [128, 512], F32) for i in range(6)]
    PT = [psum("pt6", [128, 1024], BF16), psum("pt7", [128, 1024], BF16)]

    SXs = [msem("sxa"), msem("sxb")]
    SPs = [msem("spa"), msem("spb")]
    Gs = [msem("ga"), msem("gb"), msem("gc"), msem("gd")]
    GOW = msem("gow")
    GBI = msem("gbi")
    SCN = msem("scn")
    STs = [msem("sta"), msem("stb")]
    ST = msem("st")
    PEs = msem("pe")
    VEs = msem("ve")
    SEs = msem("se")

    B = _Builder()
    op, wait = B.op, B.wait

    CM1 = cst[:, 0:1]
    CEPS = cst[:, 1:2]
    CMH = cst[:, 2:3]
    CRSQ = cst[:, 3:4]


    def slot_dma(eng, sem, dst, srcap):
        wait(eng, sem, sem.v)
        return op(eng, 'dma_start', dst, srcap, inc=sem, incn=16)

    op('v', 'memset', ones1[:], 1.0)
    op('v', 'memset', onesr[:], 1.0)
    op('v', 'memset', onescol[:], 1.0)
    op('v', 'memset', cst[:, 0:1], -1.0)
    op('v', 'memset', cst[:, 1:2], 1e-5)
    op('v', 'memset', cst[:, 2:3], -0.5)
    init_v = op('v', 'memset', cst[:, 3:4], RSQ, inc=VEs)

    op('g', 'dma_start', qb1r[:], qb_d, inc=GBI, incn=16)
    op('g', 'dma_start', kb1r[:], kb_d, inc=GBI, incn=16)
    op('g', 'dma_start', vbr[:], vb_d, inc=GBI, incn=16)
    bias_g = op('g', 'dma_start', obr[:], ob_d, inc=GBI, incn=16)

    op('y', 'dma_start', h64sb[:], h64_d, inc=SCN, incn=16)
    op('y', 'dma_start', h128sb[:], h128_d, inc=SCN, incn=16)
    consts_y = op('y', 'dma_start', idsb[:], id_d, inc=SCN, incn=16)

    wait('s', GBI, bias_g)
    op('s', 'add', qb1r[:], qb1r[:], 1.0)
    bias_s = op('s', 'add', kb1r[:], kb1r[:], 1.0, inc=SEs)
    wait('t', VEs, init_v)
    wait('s', VEs, init_v)


    xb = [_Buf(), _Buf()]
    psb = [_Buf() for _ in range(6)]
    ptb = [_Buf(), _Buf()]
    lnT_b = _Buf()
    wb_b = _Buf()
    smslot = [_Buf() for _ in range(4)]
    abslot = {}
    big0_b = _Buf(); big1_b = _Buf(); sm_b = _Buf()
    junk_b = _Buf()
    st_par = [_Buf(), _Buf(), _Buf(), _Buf()]

    XBT = [xb0, xb1]


    def ln_stats_part1(src_ap, src_buf, t, pset=None, sjunk=None):
        par = (t % 2) if pset is None else pset
        base = par * 16
        sumx = st32[:, base + 0:base + 1]
        sumq = st32[:, base + 1:base + 2]
        mneg = st32[:, base + 2:base + 3]
        v1 = st32[:, base + 3:base + 4]
        m2 = st32[:, base + 4:base + 5]
        var = st32[:, base + 5:base + 6]
        junk = ab[:, 0:2048]
        B.waits('v', st_par[par].free())
        B.waits('v', src_buf.ready())
        B.waits('v', junk_b.free())
        s1 = op('v', 'scalar_tensor_tensor', junk, src_ap, 0.0, src_ap,
                ALU.add, ALU.mult, accum_out=sumq, inc=VEs)
        junk_b.written(VEs, s1)
        sj_ap, sj_b = (big1[:], big1_b) if sjunk is None else sjunk
        B.waits('s', st_par[par].free())
        B.waits('s', src_buf.ready())
        B.waits('s', sj_b.free())
        ssum = op('s', 'activation', sj_ap, src_ap, AF.Identity,
                  accum_out=sumx, inc=SEs)
        sj_b.written(SEs, ssum)
        wait('v', VEs, s1)
        wait('v', SEs, ssum)
        op('v', 'tensor_scalar', mneg, sumx, -1.0 / D, None, ALU.mult)
        g1 = op('v', 'tensor_scalar', v1, sumq, 1.0 / D, None, ALU.mult,
                inc=VEs)
        wait('v', VEs, g1)
        g2 = op('v', 'tensor_tensor', m2, mneg, mneg, ALU.mult, inc=VEs)
        wait('v', VEs, g2)
        vstat = op('v', 'tensor_tensor', var, v1, m2, ALU.subtract, inc=VEs)
        return par, vstat

    def ln_stats_part2(src_ap, src_buf, par, vstat, out_ap):
        base = par * 16
        mneg = st32[:, base + 2:base + 3]
        var = st32[:, base + 5:base + 6]
        lnv = st32[:, base + 6:base + 7]
        rstd = st32[:, base + 7:base + 8]
        ngm = st32[:, base + 8:base + 9]
        wait('s', VEs, vstat)
        sg0 = op('s', 'activation', lnv, var, AF.Ln, bias=CEPS, inc=SEs)
        wait('s', SEs, sg0)
        sg1 = op('s', 'activation', rstd, lnv, AF.Exp, scale=CMH, inc=SEs)
        wait('s', SEs, sg1)
        sg2 = op('s', 'activation', ngm, mneg, AF.Identity, scale=rstd,
                 inc=SEs)
        wait('s', SEs, sg2)
        sdone = op('s', 'activation', out_ap, src_ap, AF.Identity,
                   bias=ngm, scale=rstd, inc=SEs)
        st_par[par].written(SEs, sdone)
        src_buf.read_by(SEs, sdone)
        return sdone

    def ln_stats_normalize(src_ap, src_buf, t, out_ap, out_dtype_note,
                           extra_s_waits=(), pset=None, sjunk=None):
        par = (t % 2) if pset is None else pset
        base = par * 16
        sumx = st32[:, base + 0:base + 1]
        sumq = st32[:, base + 1:base + 2]
        mneg = st32[:, base + 2:base + 3]
        v1 = st32[:, base + 3:base + 4]
        m2 = st32[:, base + 4:base + 5]
        var = st32[:, base + 5:base + 6]
        lnv = st32[:, base + 6:base + 7]
        rstd = st32[:, base + 7:base + 8]
        ngm = st32[:, base + 8:base + 9]
        junk = ab[:, 0:2048]
        B.waits('v', st_par[par].free())
        B.waits('v', src_buf.ready())
        B.waits('v', junk_b.free())
        s1 = op('v', 'scalar_tensor_tensor', junk, src_ap, 0.0, src_ap,
                ALU.add, ALU.mult, accum_out=sumq, inc=VEs)
        junk_b.written(VEs, s1)
        sj_ap, sj_b = (big1[:], big1_b) if sjunk is None else sjunk
        B.waits('s', st_par[par].free())
        B.waits('s', src_buf.ready())
        B.waits('s', sj_b.free())
        ssum = op('s', 'activation', sj_ap, src_ap, AF.Identity,
                  accum_out=sumx, inc=SEs)
        sj_b.written(SEs, ssum)
        wait('v', VEs, s1)
        wait('v', SEs, ssum)
        op('v', 'tensor_scalar', mneg, sumx, -1.0 / D, None, ALU.mult)
        g1 = op('v', 'tensor_scalar', v1, sumq, 1.0 / D, None, ALU.mult,
                inc=VEs)
        wait('v', VEs, g1)
        g2 = op('v', 'tensor_tensor', m2, mneg, mneg, ALU.mult, inc=VEs)
        wait('v', VEs, g2)
        vstat = op('v', 'tensor_tensor', var, v1, m2, ALU.subtract, inc=VEs)
        if DBG and t in (0, 2) and dbg_phase[0] == 'A':
            off = 0 if t == 0 else 32
            wait('y', VEs, vstat)
            op('y', 'dma_start', dbg_st[:, off:off + 16], st32[:, base:base + 16],
               inc=ST, incn=16)
        wait('s', VEs, vstat)
        for swait in extra_s_waits:
            B.waits('s', [swait])
        sg0 = op('s', 'activation', lnv, var, AF.Ln, bias=CEPS, inc=SEs)
        wait('s', SEs, sg0)
        sg1 = op('s', 'activation', rstd, lnv, AF.Exp, scale=CMH, inc=SEs)
        wait('s', SEs, sg1)
        sg2 = op('s', 'activation', ngm, mneg, AF.Identity, scale=rstd,
                 inc=SEs)
        wait('s', SEs, sg2)
        sdone = op('s', 'activation', out_ap, src_ap, AF.Identity,
                   bias=ngm, scale=rstd, inc=SEs)
        if DBG and t in (0, 2) and dbg_phase[0] == 'A':
            off = 16 if t == 0 else 48
            wait('y', SEs, sdone)
            op('y', 'dma_start', dbg_st[:, off:off + 16], st32[:, base:base + 16],
               inc=ST, incn=16)
        st_par[par].written(SEs, sdone)
        src_buf.read_by(SEs, sdone)
        return sdone

    def transpose_tile_to(dst_cols_fn, src_ap, ln_ready_val, tag):
        for k in range(4):
            bk = (tag + k) % 2
            pt = PT[bk]
            B.waits('t', ptb[bk].free())
            wait('t', SEs, ln_ready_val)
            wait('t', SCN, consts_y)
            for j in range(4):
                dt = 4 * k + j
                lastj = (j == 3)
                pe_v = op('t', 'transpose', pt[:, j * 128:(j + 1) * 128],
                          src_ap[:, dt * 128:(dt + 1) * 128], idsb[:],
                          inc=(PEs if lastj else None))
            ptb[bk].written(PEs, pe_v)
            wait('v', PEs, pe_v)
            ev = op('v', 'tensor_copy', dst_cols_fn(4 * k),
                    pt[:, 0:512], inc=VEs)
            ptb[bk].read_by(VEs, ev)
        return ev

    dbg_phase = ['X']
    xv = x_d.rearrange("(t p) d -> t p d", p=128)
    lnbuf_slots = [ab[:, 2048:4096], ab[:, 4096:6144]]
    lnbuf_b = [_Buf(), _Buf()]
    ln1T_tile_ready = []
    xs4 = [xb0, xb1, big0, sm]
    xs4_b = [xb[0], xb[1], big0_b, sm_b]
    xs4_sem = [SXs[0], SXs[1], SPs[0], SPs[1]]

    st_a = {}

    def ln1_s1(t):
        s4 = t % 3
        B.waits('y', xs4_b[s4].free())
        xw = slot_dma('y', xs4_sem[s4], xs4[s4][:], xv[t])
        xs4_b[s4].written(xs4_sem[s4], xw)
        wait('v', xs4_sem[s4], xw)
        par, vstat = ln_stats_part1(xs4[s4][:], xs4_b[s4], t, pset=t % 3)
        st_a[t] = (par, vstat, s4)

    def ln1_s2(t):
        sl = t % 2
        par, vstat, s4 = st_a[t]
        B.waits('s', lnbuf_b[sl].free())
        lnr = ln_stats_part2(xs4[s4][:], xs4_b[s4], par, vstat,
                             lnbuf_slots[sl][:, :])
        lnbuf_b[sl].written(SEs, lnr)
        st_a[t] = (par, vstat, s4, lnr)

    def ln1_s3(t):
        sl = t % 2
        lnr = st_a[t][3]
        lnTv = lnT[:].rearrange("p (dt s) -> p dt s", dt=NT)
        def dstf(dt0, t=t, lnTv=lnTv):
            return lnTv[:, dt0:dt0 + 4, t * 128:(t + 1) * 128]
        ev = transpose_tile_to(dstf, lnbuf_slots[sl], lnr, tag=t * 4)
        ln1T_tile_ready.append(ev)
        lnbuf_b[sl].read_by(PEs, PEs.v)
    if DBG:
        wait('y', VEs, ln1T_ready)
        dln = op('y', 'dma_start', dbg_ln1T, lnT[:], inc=ST, incn=16)
        lnT_b.read_by(ST, dln)

    dbg_phase[0] = 'B'
    kwv = kwT_d.rearrange("(dt p) o -> p dt o", p=128)
    vwv = vwT_d.rearrange("(dt p) o -> p dt o", p=128)
    wslot_b = [_Buf() for _ in range(4)]
    kp_t = [ab[:, 6144:6656], ab[:, 6656:7168]]
    v_t = [ab[:, 7168:7680], ab[:, 7680:8192]]
    kp_b = [_Buf(), _Buf()]
    v_b = [_Buf(), _Buf()]
    kvs_ready = None

    def proj_kv(oc, t, which):
        if which == 'k':
            bi = t % 2
            wsl = (oc % 2)
            brow = kb1r
        else:
            bi = 2 + t % 2
            wsl = 2 + (oc % 2)
            brow = vbr
        bank = PS[bi]
        B.waits('t', psb[bi].free())
        wait('t', Gs[wsl], wload_val[wsl])
        wait('t', VEs, ln1T_tile_ready[t])
        wait('t', SEs, bias_s)
        for dt in range(NT):
            op('t', 'matmul', bank[:, 0:512],
               lnT[:, dt * S + t * 128: dt * S + (t + 1) * 128],
               wb[:, wsl * 8192 + dt * 512: wsl * 8192 + (dt + 1) * 512],
               start=(dt == 0), stop=False)
        pv = op('t', 'matmul', bank[:, 0:512], ones1[:],
                brow[:, oc * 512:(oc + 1) * 512],
                start=False, stop=True, inc=PEs)
        psb[bi].written(PEs, pv)
        wslot_b[wsl].read_by(PEs, pv)
        return bi, pv

    def evac_kp(oc, t, bi, pv):
        sl = t % 2
        smi = sl
        bank = PS[bi]
        B.waits('s', smslot[smi].free())
        wait('s', PEs, pv)
        ebuf = sm[:, smi * 512:(smi + 1) * 512]
        sv = op('s', 'activation', ebuf, bank[:, 0:512], AF.Exp, bias=CM1,
                inc=SEs)
        smslot[smi].written(SEs, sv)
        B.waits('v', kp_b[sl].free())
        wait('v', SEs, sv)
        vv = op('v', 'scalar_tensor_tensor', kp_t[sl], ebuf, 1.0,
                bank[:, 0:512], ALU.min, ALU.max, inc=VEs)
        kp_b[sl].written(VEs, vv)
        smslot[smi].read_by(VEs, vv)
        psb[bi].read_by(VEs, vv)
        return vv

    def evac_v(oc, t, bi, pv):
        sl = t % 2
        bank = PS[bi]
        B.waits('v', v_b[sl].free())
        wait('v', PEs, pv)
        vv = op('v', 'tensor_copy', v_t[sl], bank[:, 0:512], inc=VEs)
        v_b[sl].written(VEs, vv)
        psb[bi].read_by(VEs, vv)
        return vv

    wload_val = [0, 0, 0, 0]
    for oc in range(4):
        for wsl, view in ((oc % 2, kwv), (2 + oc % 2, vwv)):
            B.waits('g', wslot_b[wsl].free())
            gv = slot_dma('g', Gs[wsl],
                          wb[:, wsl * 8192:(wsl + 1) * 8192]
                          .rearrange("p (dt c) -> p dt c", dt=NT),
                          view[:, :, oc * 512:(oc + 1) * 512])
            wslot_b[wsl].written(Gs[wsl], gv)
            wload_val[wsl] = gv
        d0 = 2 if oc == 0 else 0
        for t in range(NT + 1 + d0):
            if oc == 0:
                if t < NT:
                    ln1_s1(t)
                if 0 <= t - 1 < NT:
                    ln1_s2(t - 1)
                if 0 <= t - 2 < NT:
                    ln1_s3(t - 2)
            tp_ = t - d0
            if 0 <= tp_ < NT:
                bi_k, pv_k = proj_kv(oc, tp_, 'k')
                bi_v, pv_v = proj_kv(oc, tp_, 'v')
                evac_kp(oc, tp_, bi_k, pv_k)
                evac_v(oc, tp_, bi_v, pv_v)
            tt = tp_ - 1
            if tt >= 0:
                sl = tt % 2
                B.waits('t', kp_b[sl].ready())
                B.waits('t', v_b[sl].ready())
                first = (tt == 0)
                if first:
                    B.waits('t', psb[4].free())
                    B.waits('t', psb[5].free())
                last = (tt == NT - 1)
                for hh in range(4):
                    kvbank = PS[4 + hh // 2]
                    co = (hh % 2) * 256
                    st_flag = first and hh == 0
                    st_bank_first = first and (hh % 2 == 0)
                    op('t', 'matmul', kvbank[:, co:co + 128],
                       kp_t[sl][:, hh * 128:(hh + 1) * 128],
                       v_t[sl][:, hh * 128:(hh + 1) * 128],
                       start=st_bank_first, stop=last,
                       skip_group_check=True)
                    lasth = (hh == 3)
                    pv = op('t', 'matmul', kvbank[:, co + 128:co + 129],
                            kp_t[sl][:, hh * 128:(hh + 1) * 128],
                            onescol[:], start=False, stop=last,
                            skip_group_check=True,
                            inc=(PEs if lasth else None))
                kp_b[sl].read_by(PEs, pv)
                v_b[sl].read_by(PEs, pv)
                if last:
                    kvdone = pv
        wait('v', PEs, kvdone)
        for hh in range(4):
            h = oc * 4 + hh
            kvbank = PS[4 + hh // 2]
            co = (hh % 2) * 256
            vv = op('v', 'tensor_copy', kvs[:, h * 129:(h + 1) * 129],
                    kvbank[:, co:co + 129], inc=VEs)
            psb[4 + hh // 2].read_by(VEs, vv)
        kvs_ready = VEs.v
        if oc == 0:
            ln1T_ready = VEs.v
            lnT_b.written(VEs, ln1T_tile_ready[-1])

    if DBG:
        wait('y', VEs, kvs_ready)
        op('y', 'dma_start', dbg_kvs, kvs[:], inc=ST, incn=16)
    qwv = qwT_d.rearrange("(dt p) o -> p dt o", p=128)
    qsl = [ab[:, 0:2048], ab[:, 2048:4096]]
    qsl_b = [_Buf(), _Buf()]
    qpT = [ab[:, 4096:6144], ab[:, 6144:8192]]
    qpT_b = [_Buf(), _Buf()]
    att_t = [ab[:, 8192:8448], ab[:, 8448:8704]]
    att_b = [_Buf(), _Buf()]
    rz_b = _Buf()
    wb_pe_done = PEs.v
    wb_b.written(PEs, wb_pe_done)

    def emit_qw_load(h):
        sl = h % 2
        B.waits('g', qsl_b[sl].free())
        wait('g', PEs, wb_pe_done)
        wait('g', VEs, ln1T_ready)
        gv = slot_dma('g', Gs[sl],
                      qsl[sl].rearrange("p (dt c) -> p dt c", dt=NT),
                      qwv[:, :, h * 128:(h + 1) * 128])
        qsl_b[sl].written(Gs[sl], gv)
        return gv

    def emit_qp_chunk(h, gv, sc):
        sl = h % 2
        bi = sc % 3
        bank = PS[bi]
        B.waits('t', psb[bi].free())
        wait('t', Gs[sl], gv)
        for dt in range(NT):
            op('t', 'matmul', bank[:, 0:512],
               qsl[sl][:, dt * 128:(dt + 1) * 128],
               lnT[:, dt * S + sc * 512: dt * S + (sc + 1) * 512],
               start=(dt == 0), stop=False)
        pv = op('t', 'matmul', bank[:, 0:512],
                qb1r[:, h * 128:(h + 1) * 128], onesr[:],
                start=False, stop=True, inc=PEs)
        psb[bi].written(PEs, pv)
        qsl_b[sl].read_by(PEs, pv)
        smi = sc % 2
        B.waits('s', smslot[smi].free())
        wait('s', PEs, pv)
        ebuf = sm[:, smi * 512:(smi + 1) * 512]
        sv = op('s', 'activation', ebuf, bank[:, 0:512], AF.Exp, bias=CM1,
                inc=SEs)
        smslot[smi].written(SEs, sv)
        if sc == 0:
            B.waits('v', qpT_b[sl].free())
        wait('v', SEs, sv)
        vv = op('v', 'scalar_tensor_tensor',
                qpT[sl][:, sc * 512:(sc + 1) * 512], ebuf, 1.0,
                bank[:, 0:512], ALU.min, ALU.max, inc=VEs)
        smslot[smi].read_by(VEs, vv)
        psb[bi].read_by(VEs, vv)
        if sc == 3:
            qpT_b[sl].written(VEs, vv)
        return vv

    def emit_att_z(h, qp_val):
        sl = h % 2
        B.waits('t', psb[5].free())
        wait('t', VEs, qp_val)
        wait('t', VEs, kvs_ready)
        for t2 in range(NT):
            pv = op('t', 'matmul', PS[5][:, t2:t2 + 1],
                    qpT[sl][:, t2 * 128:(t2 + 1) * 128],
                    kvs[:, h * 129 + 128: h * 129 + 129],
                    start=(t2 == 0), stop=(t2 == NT - 1),
                    skip_group_check=True,
                    inc=(PEs if t2 == NT - 1 else None))
        psb[5].written(PEs, pv)
        B.waits('v', rz_b.free())
        wait('v', PEs, pv)
        tsv = op('v', 'tensor_scalar', st32[:, 32:48], PS[5][:, 0:16], 1e-6,
                 None, ALU.add, inc=VEs)
        wait('v', VEs, tsv)
        rzv = op('v', 'reciprocal', st32[:, 48:64], st32[:, 32:48], inc=VEs)
        rz_b.written(VEs, rzv)
        psb[5].read_by(VEs, rzv)
        return rzv

    def attg_mm(h, g, rzv):
        sl = h % 2
        bi = 3 + g % 2
        bank = PS[bi]
        B.waits('t', psb[bi].free())
        for j in range(2):
            t2 = 2 * g + j
            pv = op('t', 'matmul', bank[:, j * 128:(j + 1) * 128],
                    qpT[sl][:, t2 * 128:(t2 + 1) * 128],
                    kvs[:, h * 129: h * 129 + 128],
                    start=True, stop=True, skip_group_check=True,
                    inc=(PEs if j == 1 else None))
        psb[bi].written(PEs, pv)
        asl = g % 2
        B.waits('s', att_b[asl].free())
        wait('s', PEs, pv)
        wait('s', VEs, rzv)
        for j in range(2):
            t2 = 2 * g + j
            sv = op('s', 'activation', att_t[asl][:, j * 128:(j + 1) * 128],
                    bank[:, j * 128:(j + 1) * 128], AF.Identity,
                    scale=st32[:, 48 + t2:49 + t2],
                    inc=(SEs if j == 1 else None))
        att_b[asl].written(SEs, sv)
        psb[bi].read_by(SEs, sv)
        rz_b.read_by(SEs, sv)
        return sv

    def attg_tr(h, g, sv):
        asl = g % 2
        bk = g % 2
        B.waits('t', ptb[bk].free())
        wait('t', SEs, sv)
        for j in range(2):
            tv = op('t', 'transpose', PT[bk][:, j * 128:(j + 1) * 128],
                    att_t[asl][:, j * 128:(j + 1) * 128], idsb[:],
                    inc=(PEs if j == 1 else None))
        ptb[bk].written(PEs, tv)
        att_b[asl].read_by(PEs, tv)
        wait('v', PEs, tv)
        B.waits('v', wb_b.ready())
        ev = op('v', 'tensor_copy',
                wb[:, h * S + g * 256: h * S + (g + 1) * 256],
                PT[bk][:, 0:256], inc=VEs)
        ptb[bk].read_by(VEs, ev)
        return ev

    tasks = []
    attT_last = [0]

    def run_tasks(n):
        for _ in range(n):
            if not tasks:
                return
            fn = tasks.pop(0)
            fn()

    state = {}

    def mk_tasks(ph, pqv):
        def t_z():
            state['rz'] = emit_att_z(ph, pqv)
        def t_mm(g):
            def f():
                state[('sv', ph, g)] = attg_mm(ph, g, state['rz'])
            return f
        def t_tr(g):
            def f():
                attT_last[0] = attg_tr(ph, g, state[('sv', ph, g)])
                if g == 7:
                    qpT_b[ph % 2].read_by(PEs, PEs.v)
            return f
        out = [t_z, t_mm(0), t_mm(1), t_tr(0)]
        for g in range(2, 8):
            out.append(t_mm(g))
            out.append(t_tr(g - 1))
        out.append(t_tr(7))
        return out

    prev = None
    for h in range(HEADS):
        gv = emit_qw_load(h)
        for sc in range(4):
            qv = emit_qp_chunk(h, gv, sc)
            if prev is not None and sc == 0:
                tasks.extend(mk_tasks(prev[0], prev[1]))
            run_tasks(5 if len(tasks) > 12 else 4)
        prev = (h, qv)
    tasks.extend(mk_tasks(prev[0], prev[1]))
    run_tasks(len(tasks))
    attT_ready = attT_last[0]
    if DBG:
        wait('y', VEs, attT_ready)
        da = op('y', 'dma_start', dbg_att, wb[:], inc=ST, incn=16)
        wb_b.read_by(ST, da)

    lnT_pe_done = PEs.v
    owv = owT_d.rearrange("(ot p) d -> p ot d", p=128)
    lnTv2 = lnT[:].rearrange("p (ot d) -> p ot d", ot=NT)
    ow_vals = []
    for dc in range(4):
        wait('g', PEs, lnT_pe_done)
        wait('g', GOW, GOW.v)
        gv = op('g', 'dma_start', lnTv2[:, :, dc * 512:(dc + 1) * 512],
                owv[:, :, dc * 512:(dc + 1) * 512], inc=GOW, incn=16)
        ow_vals.append(gv)
    ow_ready = ow_vals[-1]

    x1v = x1h.rearrange("(t p) d -> t p d", p=128)
    x1st_val = [None] * NT
    ln2_park = [None] * NT
    o2slots = [big0, sm]
    o2_b = [big0_b, sm_b]
    for t in range(NT):
        sl = t % 2
        B.waits('y', xb[sl].free())
        xw = slot_dma('y', SXs[sl], XBT[sl][:], xv[t])
        xb[sl].written(SXs[sl], xw)
        o2 = o2slots[sl]
        o2b = o2_b[sl]
        for dc in range(4):
            bi = dc % 3
            bank = PS[bi]
            B.waits('t', psb[bi].free())
            wait('t', GOW, ow_vals[dc])
            wait('t', VEs, attT_ready)
            for ot in range(NT):
                op('t', 'matmul', bank[:, 0:512],
                   wb[:, ot * S + t * 128: ot * S + (t + 1) * 128],
                   lnT[:, ot * S + dc * 512: ot * S + (dc + 1) * 512],
                   start=(ot == 0), stop=False)
            pv = op('t', 'matmul', bank[:, 0:512], ones1[:],
                    obr[:, dc * 512:(dc + 1) * 512], start=False, stop=True,
                    inc=PEs)
            psb[bi].written(PEs, pv)
            B.waits('v', o2b.free() if dc == 0 else [])
            wait('v', PEs, pv)
            vv = op('v', 'tensor_copy', o2[:, dc * 512:(dc + 1) * 512],
                    bank[:, 0:512], inc=VEs)
            psb[bi].read_by(VEs, vv)
        o2b.written(VEs, vv)
        if DBG and t == 0:
            wait('y', VEs, vv)
            do2 = op('y', 'dma_start', dbg_o2, o2[:], inc=ST, incn=16)
            o2b.read_by(ST, do2)
        B.waits('s', big1_b.free())
        lnr = ln_stats_normalize(o2[:], o2b, t, big1[:], "f32")
        big1_b.written(SEs, lnr)
        wait('v', SEs, lnr)
        wait('v', SXs[sl], xw)
        av = op('v', 'tensor_tensor', o2[:], big1[:], XBT[sl][:], ALU.add,
                inc=VEs)
        o2b.written(VEs, av)
        big1_b.read_by(VEs, av)
        xb[sl].read_by(VEs, av)
        wait('g', VEs, av)
        stv = slot_dma('g', STs[sl], x1v[t], o2[:])
        o2b.read_by(STs[sl], stv)
        x1st_val[t] = stv
        par2, vstat2 = ln_stats_part1(o2[:], o2b, t, pset=2 + t % 2,
                                      sjunk=(XBT[sl][:], xb[sl]))
        base2 = par2 * 16
        rstd_p = st32[:, 64 + 2 * t:65 + 2 * t]
        ngm_p = st32[:, 65 + 2 * t:66 + 2 * t]
        wait('s', VEs, vstat2)
        pg0 = op('s', 'activation', st32[:, base2 + 6:base2 + 7],
                 st32[:, base2 + 5:base2 + 6], AF.Ln, bias=CEPS, inc=SEs)
        wait('s', SEs, pg0)
        pg1 = op('s', 'activation', rstd_p, st32[:, base2 + 6:base2 + 7],
                 AF.Exp, scale=CMH, inc=SEs)
        wait('s', SEs, pg1)
        pg2 = op('s', 'activation', ngm_p, st32[:, base2 + 2:base2 + 3],
                 AF.Identity, scale=rstd_p, inc=SEs)
        st_par[par2].written(SEs, pg2)
        o2b.read_by(SEs, pg2)
        ln2_park[t] = (pg2, rstd_p, ngm_p)

    lnT_pe_done2 = PEs.v
    wb_pe_done2 = PEs.v
    ln2T_tile_ready = []
    dbg_phase[0] = 'A'
    lnTv = lnT[:].rearrange("p (dt s) -> p dt s", dt=NT)
    st_e = {}
    for i in range(NT + 1):
        if i < NT:
            t = i
            sl = t % 2
            s4 = t % 4
            B.waits('y', xs4_b[s4].free())
            wait('y', STs[sl], x1st_val[t])
            xw = slot_dma('y', xs4_sem[s4], xs4[s4][:], x1v[t])
            xs4_b[s4].written(xs4_sem[s4], xw)
            pgv, rstd_p, ngm_p = ln2_park[t]
            B.waits('s', lnbuf_b[sl].free())
            wait('s', xs4_sem[s4], xw)
            wait('s', SEs, pgv)
            lnr = op('s', 'activation', lnbuf_slots[sl][:, :], xs4[s4][:],
                     AF.Identity, bias=ngm_p, scale=rstd_p, inc=SEs)
            lnbuf_b[sl].written(SEs, lnr)
            xs4_b[s4].read_by(SEs, lnr)
            st_e[t] = lnr
        if 0 <= i - 1 < NT:
            t = i - 1
            sl = t % 2
            lnr = st_e[t]
            def dstf2(dt0, t=t, lnTv=lnTv):
                return lnTv[:, dt0:dt0 + 4, t * 128:(t + 1) * 128]
            wait('v', PEs, lnT_pe_done2)
            ev = transpose_tile_to(dstf2, lnbuf_slots[sl], lnr, tag=t * 4)
            ln2T_tile_ready.append(ev)
            lnbuf_b[sl].read_by(PEs, PEs.v)
    ln2T_ready = VEs.v
    dbg_phase[0] = 'X'
    if DBG:
        wait('y', VEs, ln2T_ready)
        dl2 = op('y', 'dma_start', dbg_ln2T, lnT[:], inc=ST, incn=16)
        lnT_b.written(VEs, ln2T_ready)
        lnT_b.read_by(ST, dl2)

    w1v = w1T_d.rearrange("(dt p) h -> p dt h", p=128)
    w3v = w3T_d.rearrange("(dt p) h -> p dt h", p=128)
    def wbu(u, a=0, b=2048):
        return wb[:, u * 2048 + a: u * 2048 + b]
    wfb = {k: _Buf() for k in ['w10', 'w11', 'w30', 'w31', 'hb', 'hg0', 'hg1']}
    pend_h64 = [None]
    wait('s', STs[0], STs[0].v)
    wait('s', STs[1], STs[1].v)
    for m in range(64):
        s1 = m % 2
        w1slot, w1b = (0 if s1 == 0 else 1), wfb['w1' + str(s1)]
        w3slot, w3b = (2 if s1 == 0 else 3), wfb['w3' + str(s1)]
        B.waits('g', w1b.free())
        wait('g', PEs, wb_pe_done2)
        g1 = slot_dma('g', Gs[w1slot],
                      wbu(w1slot).rearrange("p (dt c) -> p dt c", dt=NT),
                      w1v[:, :, m * 128:(m + 1) * 128])
        w1b.written(Gs[w1slot], g1)
        B.waits('g', w3b.free())
        g3 = slot_dma('g', Gs[w3slot],
                      wbu(w3slot).rearrange("p (dt c) -> p dt c", dt=NT),
                      w3v[:, :, m * 128:(m + 1) * 128])
        w3b.written(Gs[w3slot], g3)
        hgslot = 5 + m % 2
        hgb = wfb['hg' + str(m % 2)]
        for sc in range(4):
            big = sc % 2
            bg, bl = PS[big], PS[2 + big]
            B.waits('t', psb[big].free())
            wait('t', Gs[w1slot], g1)
            wait('t', VEs, ln2T_tile_ready[4 * sc + 3])
            for dt in range(NT):
                op('t', 'matmul', bg[:, 0:512],
                   wbu(w1slot, dt * 128, (dt + 1) * 128),
                   lnT[:, dt * S + sc * 512: dt * S + (sc + 1) * 512],
                   start=(dt == 0), stop=(dt == NT - 1),
                   inc=(PEs if dt == NT - 1 else None))
            pg = PEs.v
            psb[big].written(PEs, pg)
            w1b.read_by(PEs, pg)
            B.waits('t', psb[2 + big].free())
            wait('t', Gs[w3slot], g3)
            for dt in range(NT):
                op('t', 'matmul', bl[:, 0:512],
                   wbu(w3slot, dt * 128, (dt + 1) * 128),
                   lnT[:, dt * S + sc * 512: dt * S + (sc + 1) * 512],
                   start=(dt == 0), stop=(dt == NT - 1),
                   inc=(PEs if dt == NT - 1 else None))
            pl = PEs.v
            psb[2 + big].written(PEs, pl)
            w3b.read_by(PEs, pl)
            if pend_h64[0] is not None:
                pend_h64[0]()
                pend_h64[0] = None
            smi = 2 + sc % 2
            B.waits('s', smslot[smi].free())
            wait('s', PEs, pg)
            sbuf_ = sm[:, smi * 512:(smi + 1) * 512]
            sv = op('s', 'activation', sbuf_, bg[:, 0:512], AF.Silu, inc=SEs)
            smslot[smi].written(SEs, sv)
            psb[big].read_by(SEs, sv)
            B.waits('v', wfb['hb'].free() if sc == 0 else [])
            wait('v', SEs, sv)
            wait('v', PEs, pl)
            hv = op('v', 'tensor_tensor', wbu(4, sc * 512, (sc + 1) * 512),
                    sbuf_, bl[:, 0:512], ALU.mult, inc=VEs)
            smslot[smi].read_by(VEs, hv)
            psb[2 + big].read_by(VEs, hv)
            def mk_h64(sc=sc, hv=hv, hgslot=hgslot, hgb=hgb, m=m):
                bi6 = 4 + sc % 2
                bank = PS[bi6]
                B.waits('t', psb[bi6].free())
                wait('t', VEs, hv)
                wait('t', SCN, consts_y)
                pv = op('t', 'matmul', bank[:, 0:512], h64sb[:],
                        wbu(4, sc * 512, (sc + 1) * 512),
                        start=True, stop=True, inc=PEs)
                psb[bi6].written(PEs, pv)
                wfb['hb'].read_by(PEs, pv)
                if sc == 0:
                    B.waits('v', hgb.free())
                wait('v', PEs, pv)
                ev = op('v', 'tensor_copy',
                        wbu(hgslot, sc * 512, (sc + 1) * 512),
                        bank[:, 0:512], inc=VEs)
                psb[bi6].read_by(VEs, ev)
                if sc == 3:
                    hgb.written(VEs, ev)
                    wait('y', VEs, ev)
                    dst = hidh[:, 2 * m:2 * m + 2, :, :].rearrange(
                        "t a b s -> (a b) t s")
                    stv = slot_dma('y', STs[m % 2], dst,
                                   wbu(hgslot).rearrange("p (t s) -> p t s",
                                                         t=NT))
                    hgb.read_by(STs[m % 2], stv)
            pend_h64[0] = mk_h64
        if m == 63:
            pend_h64[0]()
            pend_h64[0] = None
        wfb['hb'].written(VEs, VEs.v)

    w2v = w2g_d.rearrange("(b p) d -> p b d", p=128)
    parkv = parkh.rearrange("(t p) d -> t p d", p=128)
    wraw = [ab[:, 0:4096], ab[:, 4096:8192]]
    wraw_b = [_Buf(), _Buf()]
    stripe = [lnT[:, 0:8192], lnT[:, 8192:16384]]
    stripe_b = [_Buf(), _Buf()]
    park_sl = [sm[:, 0:512], sm[:, 512:1024]]
    park_b = [smslot[0], smslot[1]]
    lnT_pe_done3 = PEs.v
    wb_pe_done3 = PEs.v
    w2q_ready = 0
    wb_part_free = {}
    outv = out_d.rearrange("(t p) d -> t p d", p=128)
    fslots = [big0, sm]
    fslots_b = [big0_b, sm_b]
    hidst_done = (STs[0].v, STs[1].v)
    wait('s', STs[0], hidst_done[0])
    wait('s', STs[1], hidst_done[1])
    parkst = [[None] * NT for _ in range(3)]
    GM = [msem("gm0"), msem("gm1")]
    for q in range(4):
        wb_free_val = PEs.v
        wb_part_free_prev = dict(wb_part_free)
        wb_part_free = {}
        if q == 3:
            park_q2_done = (STs[0].v, STs[1].v)
            pend_store = []
        bb_ready = [0] * 8
        for bb in range(8):
            sl = bb % 2
            B.waits('g', wraw_b[sl].free())
            wait('g', VEs, ln2T_ready)
            gv = slot_dma('g', Gs[sl],
                          wraw[sl].rearrange("p (j c) -> p j c", j=8),
                          w2v[:, bb * 8:(bb + 1) * 8, q * 512:(q + 1) * 512])
            wraw_b[sl].written(Gs[sl], gv)
            for j in range(8):
                b = bb * 8 + j
                bi = j % 2
                bank = PS[bi]
                B.waits('t', psb[bi].free())
                wait('t', Gs[sl], gv)
                wait('t', SCN, consts_y)
                pv = op('t', 'matmul', bank[:, 0:512], h128sb[:],
                        wraw[sl][:, j * 512:(j + 1) * 512],
                        start=True, stop=True, inc=PEs)
                psb[bi].written(PEs, pv)
                wraw_b[sl].read_by(PEs, pv)
                wait('s', PEs, pv)
                part = min(b // 16, 3)
                gate = wb_part_free_prev.get(part + 1, wb_free_val)
                wait('s', PEs, max(wb_pe_done3, gate if part < 3 else wb_free_val))
                wait('s', VEs, init_v)
                sv = op('s', 'activation', wb[:, b * 512:(b + 1) * 512],
                        bank[:, 0:512], AF.Identity, scale=CRSQ, inc=SEs)
                psb[bi].read_by(SEs, sv)
            bb_ready[bb] = sv
        w2q_ready = sv
        for t in range(NT):
            sl = t % 2
            B.waits('y', stripe_b[sl].free())
            wait('y', STs[0], hidst_done[0])
            wait('y', STs[1], hidst_done[1])
            wait('y', PEs, lnT_pe_done3)
            lw = slot_dma('y', SPs[sl], stripe[sl], hidh[t].rearrange("a b s -> a (b s)"))
            stripe_b[sl].written(SPs[sl], lw)
            bi = 2 + t % 2
            bank = PS[bi]
            B.waits('t', psb[bi].free())
            wait('t', SPs[sl], lw)
            wait('t', PEs, lnT_pe_done3)
            last_t = (t == NT - 1)
            for b in range(64):
                if b % 8 == 0:
                    wait('t', SEs, bb_ready[b // 8])
                mark = last_t and (b in (15, 31, 47))
                op('t', 'matmul', bank[:, 0:512],
                   stripe[sl][:, b * 128:(b + 1) * 128],
                   wb[:, b * 512:(b + 1) * 512],
                   start=(b == 0), stop=(b == 63),
                   inc=(PEs if (b == 63 or mark) else None))
                if mark:
                    wb_part_free[(b + 1) // 16] = PEs.v
            pv = PEs.v
            psb[bi].written(PEs, pv)
            stripe_b[sl].read_by(PEs, pv)
            if q < 3:
                psl = t % 2
                B.waits('v', park_b[psl].free())
                wait('v', PEs, pv)
                ev = op('v', 'tensor_copy', park_sl[psl], bank[:, 0:512],
                        inc=VEs)
                psb[bi].read_by(VEs, ev)
                park_b[psl].written(VEs, ev)
                wait('s', VEs, ev)
                stv = slot_dma('s', STs[psl],
                               parkv[t][:, q * 512:(q + 1) * 512],
                               park_sl[psl])
                park_b[psl].read_by(STs[psl], stv)
                parkst[q][t] = stv
            else:
                fsl = t % 2
                fb = fslots_b[fsl]
                B.waits('g', fb.free())
                wait('g', STs[0], park_q2_done[0])
                wait('g', STs[1], park_q2_done[1])
                pw = slot_dma('g', GM[fsl], fslots[fsl][:, 0:1536],
                              parkv[t][:, 0:1536])
                B.waits('g', xb[fsl].free())
                xw = slot_dma('g', SXs[fsl], XBT[fsl][:], x1v[t])
                xb[fsl].written(SXs[fsl], xw)
                wait('v', GM[fsl], pw)
                wait('v', PEs, pv)
                ev = op('v', 'tensor_copy', fslots[fsl][:, 1536:2048],
                        bank[:, 0:512], inc=VEs)
                psb[bi].read_by(VEs, ev)
                fb.written(VEs, ev)
                B.waits('s', big1_b.free())
                wait('s', GM[fsl], pw)
                lnr = ln_stats_normalize(fslots[fsl][:], fb, t, big1[:],
                                         "f32")
                big1_b.written(SEs, lnr)
                wait('v', SEs, lnr)
                wait('v', SXs[fsl], xw)
                av = op('v', 'tensor_tensor', fslots[fsl][:], big1[:],
                        XBT[fsl][:], ALU.add, inc=VEs)
                fb.written(VEs, av)
                big1_b.read_by(VEs, av)
                xb[fsl].read_by(VEs, av)
                pend_store.append((t, fsl, av))
                if len(pend_store) >= 2:
                    t_, fsl_, av_ = pend_store.pop(0)
                    wait('g', VEs, av_)
                    stv = slot_dma('g', ST, outv[t_], fslots[fsl_][:])
                    fslots_b[fsl_].read_by(ST, stv)
    for t_, fsl_, av_ in pend_store:
        wait('g', VEs, av_)
        stv = slot_dma('g', ST, outv[t_], fslots[fsl_][:])
        fslots_b[fsl_].read_by(ST, stv)
    wait('y', ST, ST.v)
    wait('y', STs[0], STs[0].v)
    wait('y', STs[1], STs[1].v)

    with nc.Block() as block:
        @block.tensor
        def _(e):
            for f in B.prog['t']:
                f(e)

        @block.vector
        def _(e):
            for f in B.prog['v']:
                f(e)

        @block.scalar
        def _(e):
            for f in B.prog['s']:
                f(e)

        @block.sync
        def _(e):
            for f in B.prog['y']:
                f(e)

        @block.gpsimd
        def _(e):
            for f in B.prog['g']:
                f(e)
    return nc


def _prep_inputs(L0_attn_hidden_states, L0_attn_q_w, L0_attn_q_b, L0_attn_k_w,
                 L0_attn_k_b, L0_attn_v_w, L0_attn_v_b, L0_attn_o_w,
                 L0_attn_o_b, L0_ffn_W1, L0_ffn_W2, L0_ffn_W3):
    c = np.ascontiguousarray
    f = np.float32
    qwT = c(np.asarray(L0_attn_q_w, f).T)
    kwT = c(np.asarray(L0_attn_k_w, f).T)
    vwT = c(np.asarray(L0_attn_v_w, f).T)
    owT = c(np.asarray(L0_attn_o_w, f).T)
    w1T = c(np.asarray(L0_ffn_W1, f).T)
    w3T = c(np.asarray(L0_ffn_W3, f).T)
    w2g = c(np.asarray(L0_ffn_W2, f).T.reshape(128, 64, D).transpose(1, 0, 2)
            .reshape(HF, D))
    qb = c(np.asarray(L0_attn_q_b, f).reshape(1, D))
    kb = c(np.asarray(L0_attn_k_b, f).reshape(1, D))
    vb = c(np.asarray(L0_attn_v_b, f).reshape(1, D))
    ob = c(np.asarray(L0_attn_o_b, f).reshape(1, D))
    h64 = _had(64)
    h64i2 = np.kron(np.eye(2, dtype=np.float32), h64).astype(ml_dtypes.bfloat16)
    h128 = _had(128).astype(ml_dtypes.bfloat16)
    ident = np.eye(128, dtype=ml_dtypes.bfloat16)
    shared = dict(qwT=qwT, kwT=kwT, vwT=vwT, owT=owT, qb=qb, kb=kb, vb=vb,
                  ob=ob, w1T=w1T, w3T=w3T, w2g=w2g, h64i2=h64i2, h128=h128,
                  ident=ident)
    xs = np.asarray(L0_attn_hidden_states, f)
    in_maps = []
    for core in range(8):
        m = dict(shared)
        m["x"] = c(xs[core])
        in_maps.append(m)
    return in_maps


def kernel(**inputs):
    in_maps = _prep_inputs(**inputs)
    nc = build_nc()
    res = run_bass_kernel_spmd(nc, in_maps, core_ids=list(range(8)))
    out = np.stack([r["out"] for r in res.results], axis=0)
    return out.astype(np.float32)
